# version 18
# baseline (speedup 1.0000x reference)
import numpy as np
import ml_dtypes
from contextlib import ExitStack

import concourse.bass as bass
import concourse.mybir as mybir

F32 = mybir.dt.float32
BF16 = mybir.dt.bfloat16
I32 = mybir.dt.int32
AF = mybir.ActivationFunctionType
ALU = mybir.AluOpType
AX = mybir.AxisListType


class Cfg:
    def __init__(self, S=2048, D=1024, H=4096, NSEQ=2, T1=256, T2=512):
        self.S, self.D, self.H, self.NSEQ = S, D, H, NSEQ
        self.DC, self.HC = D // 128, H // 128
        self.T1, self.T2 = T1, T2
        self.G = min(512, S)
        self.KBW = min(512, S)
        self.KB = S // self.KBW
        self.T = NSEQ * S
        self.ST = S // 128
        self.NGRP = S // self.G
        self.NQT = self.G // 128
        self.LN_EPS = 1e-5
        self.QSCALE = float(D) ** (-0.5)


class Sem:
    def __init__(self, h):
        self.h, self.n = h, 0


class PB:
    ENGS = ("sync", "scalar", "vector", "tensor", "gpsimd")

    def __init__(self, nc, name=""):
        self.nc, self.name = nc, name
        self.items = {e: [] for e in self.ENGS}

    def op(self, eng, fn, inc=None, k=1, drain=None):
        if drain is None:
            drain = eng in ("scalar", "vector", "gpsimd")
        tgt = None
        if inc is not None:
            inc.n += k
            tgt = (inc.h, k)

        def cl(e, fn=fn, tgt=tgt, drain=drain):
            ins = fn(e)
            if tgt is not None:
                ins.then_inc(*tgt)
            if drain:
                e.drain()

        self.items[eng].append(cl)

    def wait(self, eng, sem, val):
        self.items[eng].append(lambda e, h=sem.h, v=val: e.wait_ge(h, v))

    def run(self):
        with self.nc.Block() as block:
            for name in self.ENGS:
                fns = self.items[name]
                if not fns:
                    continue

                def body(e, fns=fns):
                    for f in fns:
                        f(e)

                getattr(block, name)(body)


def _ap(handle, offset, dims):
    return bass.AP(handle, offset, dims)


def build_program(cfg: Cfg, debug=False):
    nc = bass.Bass("TRN2", target_bir_lowering=False)
    S, D, H, DC, HC = cfg.S, cfg.D, cfg.H, cfg.DC, cfg.HC
    T, G, KB, KBW = cfg.T, cfg.G, cfg.KB, cfg.KBW
    T1, T2, ST = cfg.T1, cfg.T2, cfg.ST
    NGRP, NQT, NSEQ = cfg.NGRP, cfg.NQT, cfg.NSEQ
    NT1, NT2 = T // T1, T // T2
    EPS = cfg.LN_EPS
    DH = min(512, D)
    NDH = D // DH
    NPTB = (ST + 3) // 4

    def din(name, shape):
        return nc.dram_tensor(name, list(shape), F32, kind="ExternalInput").ap()

    x_tm = din("x_tm", (T, D))
    x_fm = din("x_fm", (NSEQ, D, S))
    wqT = din("wqT", (D, D))
    wkT = din("wkT", (D, D))
    wvT = din("wvT", (D, D))
    wfcT = din("wfcT", (D, D))
    bfc_d = din("bfc", (1, D))
    gamma_d = din("gamma", (1, D))
    beta_d = din("beta", (1, D))
    w1T = din("w1T", (D, H))
    w2T = din("w2T", (H, D))
    cw_d = din("cw", (H, 3))
    cb_d = din("cb", (H, 1))

    y_out = nc.dram_tensor("y", [T, D], F32, kind="ExternalOutput").ap()
    ik = "ExternalOutput" if debug else "Internal"
    x1s = nc.dram_tensor("x1s", [T, D], F32, kind=ik).ap()
    xqs = nc.dram_tensor("xqs", [T, D], BF16, kind=ik).ap()
    hqs = nc.dram_tensor("hqs", [NT1, HC, 128, T1], BF16, kind=ik).ap()
    f1s = nc.dram_tensor("f1s", [T, 1], F32, kind=ik).ap()
    vrow_d = nc.dram_tensor("vrow_d", [1, D], F32, kind="Internal").ap()
    s2row = nc.dram_tensor("s2row", [2, T1], F32, kind="Internal").ap()
    if debug:
        dbg_u = nc.dram_tensor("dbg_u", [H, T], F32, kind="ExternalOutput").ap()
        dbg_mx = nc.dram_tensor("dbg_mx", [T, 2], F32, kind="ExternalOutput").ap()
        dbg_z = nc.dram_tensor("dbg_z", [H, T], F32, kind="ExternalOutput").ap()
        dbg_i = nc.dram_tensor("dbg_i", [H, T], I32, kind="ExternalOutput").ap()

    ident_d = nc.inline_tensor(np.eye(128, dtype=ml_dtypes.bfloat16), "c_ident")
    identf_d = nc.inline_tensor(np.eye(128, dtype=np.float32), "c_identf")
    esel_np = np.zeros((128, 128), dtype=np.float32)
    esel_np[0, :] = 1.0
    esel_d = nc.inline_tensor(esel_np, "c_esel")
    ones_d = nc.inline_tensor(np.ones((128, 1), dtype=np.float32), "c_ones")
    c128_d = nc.inline_tensor(np.full((128, 1), 128.0, dtype=np.float32),
                              "c_c128")

    stack = ExitStack()
    E = stack.enter_context

    def sb(name, shape, dtype):
        return E(nc.sbuf_tensor(name, list(shape), dtype))

    ident = sb("ident", (128, 128), BF16)
    identf = sb("identf", (128, 128), F32)
    esel = sb("esel", (128, 128), F32)
    onescol = sb("onescol", (128, 1), F32)
    c128 = sb("c128", (128, 1), F32)
    gammaBC = sb("gammaBC", (128, D), F32)
    betaBC = sb("betaBC", (128, D), F32)
    bfcBC = sb("bfcBC", (128, D), F32)
    rowbuf = sb("rowbuf", (128, 512), F32)
    cwcol = sb("cwcol", (128, 3 * HC), F32)
    cbcol = sb("cbcol", (128, HC), F32)
    f2cols = sb("f2cols", (128, T // 128), F32)
    wsmat = sb("wsmat", (128, 16), F32)
    wqcols = sb("wqcols", (128, 8), F32)
    cols = sb("cols", (128, 48), F32)
    colsb = sb("colsb", (128, 48), F32)
    wbuf = sb("wbuf", (128, DC * D), BF16)

    ps_s = [E(nc.psum_tensor(f"ps_s{i}", [128, 512], F32)) for i in range(4)]
    ps_a = E(nc.psum_tensor("ps_a", [128, 512], F32))
    ps_b = E(nc.psum_tensor("ps_b", [128, 512], F32))
    ps_c = E(nc.psum_tensor("ps_c", [128, 512], F32))
    ps_d = E(nc.psum_tensor("ps_d", [128, 512], F32))

    dS = Sem(E(nc.semaphore("dS")))
    dSb = Sem(E(nc.semaphore("dSb")))
    dOb = Sem(E(nc.semaphore("dOb")))
    dX = Sem(E(nc.semaphore("dX")))
    dXb = Sem(E(nc.semaphore("dXb")))
    dG = Sem(E(nc.semaphore("dG")))
    dO = Sem(E(nc.semaphore("dO")))
    mm = Sem(E(nc.semaphore("mm")))
    vv = Sem(E(nc.semaphore("vv")))
    ssm = Sem(E(nc.semaphore("ssm")))
    gg = Sem(E(nc.semaphore("gg")))
    sA = Sem(E(nc.semaphore("sA")))
    sB = Sem(E(nc.semaphore("sB")))

    p = PB(nc, "P0")
    for dst, srcd in ((ident, ident_d), (identf, identf_d), (esel, esel_d),
                      (onescol, ones_d), (c128, c128_d)):
        p.op("sync", lambda s, dst=dst, srcd=srcd: s.dma_start(
            out=dst[:], in_=srcd.ap()), inc=dS, k=16)
    p.op("sync", lambda s: s.dma_start(
        out=gammaBC[:], in_=_ap(gamma_d.tensor, 0, [[0, 128], [1, D]])),
        inc=dS, k=16)
    p.op("sync", lambda s: s.dma_start(
        out=betaBC[:], in_=_ap(beta_d.tensor, 0, [[0, 128], [1, D]])),
        inc=dS, k=16)
    p.op("sync", lambda s: s.dma_start(
        out=bfcBC[:], in_=_ap(bfc_d.tensor, 0, [[0, 128], [1, D]])),
        inc=dS, k=16)
    for c in range(HC):
        p.op("sync", lambda s, c=c: s.dma_start(
            out=cwcol[:, 3 * c:3 * c + 3], in_=cw_d[c * 128:(c + 1) * 128, :]),
            inc=dS, k=16)
    for c in range(HC):
        p.op("sync", lambda s, c=c: s.dma_start(
            out=cbcol[:, c:c + 1], in_=cb_d[c * 128:(c + 1) * 128, :]),
            inc=dS, k=16)
    p.wait("sync", dS, dS.n)
    p.op("vector", lambda v: v.memset(rowbuf[:], 0.0))
    p.op("vector", lambda v: v.memset(wsmat[:], 0.0))
    p.run()

    with nc.sbuf_tensor("wstage", [128, H], F32) as wstage, \
         nc.sbuf_tensor("wstage2", [128, H], F32) as wstage2, \
         nc.sbuf_tensor("wabs", [128, H], F32) as wabs:
        p = PB(nc, "W0")
        p.wait("sync", dS, dS.n)
        p.wait("sync", dSb, dSb.n)
        chunks = [(w1T[dc * 128:(dc + 1) * 128, :], H) for dc in range(DC)] + \
                 [(w2T[hc * 128:(hc + 1) * 128, :], D) for hc in range(HC)]
        NCH = len(chunks)
        for i, (src, width) in enumerate(chunks):
            stage = wstage if i % 2 == 0 else wstage2
            sem = dS if i % 2 == 0 else dSb
            if i >= 2:
                p.wait("sync", ssm, ssm.n - 1)
            p.op("sync", lambda s, stage=stage, src=src, width=width:
                 s.dma_start(out=stage[:, 0:width], in_=src), inc=sem, k=16)
            p.wait("scalar", sem, sem.n)
            if i >= 1:
                p.wait("scalar", vv, vv.n)
            p.op("scalar", lambda sc, stage=stage, width=width: sc.activation(
                out=wabs[:, 0:width], in_=stage[:, 0:width], func=AF.Abs), inc=ssm)
            p.wait("vector", ssm, ssm.n)
            p.op("vector", lambda v, i=i, width=width: v.reduce_sum(
                out=colsb[:, i:i + 1], in_=wabs[:, 0:width], axis=AX.X), inc=vv)
        p.op("vector", lambda v: v.reduce_sum(
            out=cols[:, 0:1], in_=colsb[:, 0:DC], axis=AX.X))
        p.op("vector", lambda v: v.reduce_sum(
            out=cols[:, 1:2], in_=colsb[:, DC:DC + HC], axis=AX.X), inc=vv)
        p.wait("tensor", vv, vv.n)
        p.op("tensor", lambda t: t.matmul(ps_c[0:1, 0:1], onescol[:], cols[:, 0:1],
                                          start=True, stop=True))
        p.op("tensor", lambda t: t.matmul(ps_c[0:1, 1:2], onescol[:], cols[:, 1:2],
                                          start=True, stop=True), inc=mm)
        p.wait("vector", mm, mm.n)
        p.op("vector", lambda v: v.tensor_copy(out=wsmat[0:1, 0:2], in_=ps_c[0:1, 0:2]))
        p.op("vector", lambda v: v.tensor_scalar(
            out=wsmat[0:1, 2:3], in0=wsmat[0:1, 0:1], scalar1=1.0 / (D * H),
            scalar2=1e-5, op0=ALU.mult, op1=ALU.max))
        p.op("vector", lambda v: v.tensor_scalar(
            out=wsmat[0:1, 3:4], in0=wsmat[0:1, 1:2], scalar1=1.0 / (D * H),
            scalar2=1e-5, op0=ALU.mult, op1=ALU.max))
        p.op("vector", lambda v: v.reciprocal(out=wsmat[0:1, 4:5], in_=wsmat[0:1, 2:3]))
        p.op("vector", lambda v: v.reciprocal(out=wsmat[0:1, 5:6], in_=wsmat[0:1, 3:4]))
        p.op("vector", lambda v: v.reciprocal(out=wsmat[0:1, 6:7], in_=wsmat[0:1, 4:5]))
        p.op("vector", lambda v: v.reciprocal(out=wsmat[0:1, 7:8], in_=wsmat[0:1, 5:6]))
        p.op("vector", lambda v: v.tensor_scalar(
            out=wsmat[0:1, 8:9], in0=wsmat[0:1, 6:7], scalar1=-1.0, scalar2=None,
            op0=ALU.mult))
        p.op("vector", lambda v: v.tensor_scalar(
            out=wsmat[0:1, 9:10], in0=wsmat[0:1, 7:8], scalar1=-1.0, scalar2=None,
            op0=ALU.mult), inc=vv)
        p.wait("tensor", vv, vv.n)
        for j, cell in enumerate((4, 5, 8, 9)):
            p.op("tensor", lambda t, j=j, cell=cell: t.matmul(
                ps_d[:, j:j + 1], esel[:], wsmat[:, cell:cell + 1],
                start=True, stop=True), inc=(mm if j == 3 else None))
        p.wait("vector", mm, mm.n)
        p.op("vector", lambda v: v.tensor_copy(out=wqcols[:, 0:4], in_=ps_d[:, 0:4]),
             inc=vv)
        p.run()

    def emit_epilogue(p, row0, psA, psB, mm_wait, eb, kind, par, dep_store=None,
                      sto=None, ldsem=None):
        xres, htile, atile, ln1t, sqsc, x1t, iqt, iqtb, xqt = eb
        if sto is None:
            sto = dO
        c0 = 24 * par
        xr = xres[:, par * D:(par + 1) * D]
        x1 = x1t[:, par * D:(par + 1) * D]
        xq = xqt[:, par * D:(par + 1) * D]

        def C(j):
            return cols[:, c0 + j:c0 + j + 1]

        p.wait("vector", mm, mm_wait)
        p.wait("vector", ldsem, ldsem.n)
        if kind == "ffn":
            f2c = f2cols[:, row0 // 128: row0 // 128 + 1]
            p.op("vector", lambda v: v.tensor_scalar(
                out=htile[:, 0:DH], in0=psA[:, 0:DH], scalar1=f2c, scalar2=None,
                op0=ALU.mult))
            if NDH > 1:
                p.op("vector", lambda v: v.tensor_scalar(
                    out=htile[:, DH:D], in0=psB[:, 0:DH], scalar1=f2c, scalar2=None,
                    op0=ALU.mult))
            p.op("vector", lambda v: v.tensor_tensor(
                out=atile[:], in0=htile[:], in1=xr, op=ALU.add), inc=vv)
            ev_h1 = vv.n
        else:
            p.op("vector", lambda v: v.tensor_tensor(
                out=htile[:], in0=atile[:], in1=bfcBC[:], op=ALU.add))
            p.op("vector", lambda v: v.tensor_tensor(
                out=atile[:], in0=htile[:], in1=xr, op=ALU.add), inc=vv)
            ev_h1 = vv.n
        p.op("vector", lambda v: v.reduce_sum(out=C(0), in_=atile[:], axis=AX.X))
        p.wait("scalar", vv, ev_h1)
        p.op("scalar", lambda sc: sc.activation(
            out=sqsc[:], in_=atile[:], func=AF.Square, accum_out=C(1)), inc=ssm)
        ss_sq1 = ssm.n
        p.wait("vector", ssm, ss_sq1)
        p.op("vector", lambda v: v.tensor_scalar(
            out=C(2), in0=C(0), scalar1=1.0 / D, scalar2=None, op0=ALU.mult))
        p.op("vector", lambda v: v.tensor_scalar(
            out=C(3), in0=C(1), scalar1=1.0 / D, scalar2=EPS, op0=ALU.mult,
            op1=ALU.add))
        p.op("vector", lambda v: v.tensor_scalar(
            out=C(4), in0=C(2), scalar1=C(2), scalar2=-1.0, op0=ALU.mult,
            op1=ALU.mult))
        p.op("vector", lambda v: v.tensor_tensor(
            out=C(5), in0=C(3), in1=C(4), op=ALU.add), inc=vv)
        ev_var1 = vv.n
        p.wait("scalar", vv, ev_var1)
        p.op("scalar", lambda sc: sc.activation(out=C(6), in_=C(5), func=AF.Ln))
        p.op("scalar", lambda sc: sc.activation(out=C(7), in_=C(6), func=AF.Exp,
                                                scale=-0.5), inc=ssm)
        ss_sd1 = ssm.n
        p.wait("vector", ssm, ss_sd1)
        if kind == "ffn":
            if dep_store is not None:
                p.wait("vector", sto, dep_store)
            p.op("vector", lambda v: v.tensor_scalar(
                out=x1, in0=atile[:], scalar1=C(2), scalar2=C(7),
                op0=ALU.subtract, op1=ALU.mult), inc=vv)
            ev_out = vv.n
            p.wait("scalar", vv, ev_out)
            p.op("scalar", lambda sc: sc.dma_start(
                out=y_out[row0:row0 + 128, :], in_=x1), inc=sto, k=16)
            return ev_h1
        p.op("vector", lambda v: v.tensor_scalar(
            out=ln1t[:], in0=atile[:], scalar1=C(2), scalar2=C(7),
            op0=ALU.subtract, op1=ALU.mult))
        p.op("vector", lambda v: v.tensor_tensor(
            out=htile[:], in0=ln1t[:], in1=gammaBC[:], op=ALU.mult))
        p.op("vector", lambda v: v.tensor_tensor(
            out=ln1t[:], in0=htile[:], in1=betaBC[:], op=ALU.add))
        p.op("vector", lambda v: v.tensor_tensor(
            out=htile[:], in0=ln1t[:], in1=xr, op=ALU.add), inc=vv)
        ev_h2 = vv.n
        p.op("vector", lambda v: v.reduce_sum(out=C(8), in_=htile[:], axis=AX.X))
        p.wait("scalar", vv, ev_h2)
        p.op("scalar", lambda sc: sc.activation(
            out=sqsc[:], in_=htile[:], func=AF.Square, accum_out=C(9)), inc=ssm)
        ss_sq2 = ssm.n
        p.wait("vector", ssm, ss_sq2)
        p.op("vector", lambda v: v.tensor_scalar(
            out=C(10), in0=C(8), scalar1=1.0 / D, scalar2=None, op0=ALU.mult))
        p.op("vector", lambda v: v.tensor_scalar(
            out=C(11), in0=C(9), scalar1=1.0 / D, scalar2=EPS, op0=ALU.mult,
            op1=ALU.add))
        p.op("vector", lambda v: v.tensor_scalar(
            out=C(12), in0=C(10), scalar1=C(10), scalar2=-1.0, op0=ALU.mult,
            op1=ALU.mult))
        p.op("vector", lambda v: v.tensor_tensor(
            out=C(13), in0=C(11), in1=C(12), op=ALU.add), inc=vv)
        ev_var2 = vv.n
        p.wait("scalar", vv, ev_var2)
        p.op("scalar", lambda sc: sc.activation(out=C(14), in_=C(13), func=AF.Ln))
        p.op("scalar", lambda sc: sc.activation(out=C(15), in_=C(14), func=AF.Exp,
                                                scale=-0.5), inc=ssm)
        ss_sd2 = ssm.n
        p.wait("vector", ssm, ss_sd2)
        if dep_store is not None:
            p.wait("vector", sto, dep_store)
        p.op("vector", lambda v: v.tensor_scalar(
            out=x1, in0=htile[:], scalar1=C(10), scalar2=C(15),
            op0=ALU.subtract, op1=ALU.mult), inc=vv)
        ev_x1 = vv.n
        p.wait("scalar", vv, ev_x1)
        p.op("scalar", lambda sc: sc.dma_start(
            out=x1s[row0:row0 + 128, :], in_=x1), inc=sto, k=16)
        p.op("vector", lambda v: v.reduce_max(
            out=C(16), in_=x1, axis=AX.X, apply_absolute_value=True))
        p.op("vector", lambda v: v.tensor_scalar(
            out=C(17), in0=C(16), scalar1=1e-5, scalar2=None, op0=ALU.max))
        p.op("vector", lambda v: v.reciprocal(out=C(18), in_=C(17)))
        p.op("vector", lambda v: v.tensor_scalar(
            out=C(19), in0=C(18), scalar1=127.0, scalar2=None, op0=ALU.mult))
        p.op("vector", lambda v: v.reciprocal(out=C(20), in_=C(19)), inc=vv)
        ev_scale = vv.n
        p.wait("scalar", vv, ev_scale)
        p.op("scalar", lambda sc: sc.dma_start(
            out=f1s[row0:row0 + 128, :], in_=C(20)), inc=sto, k=16)
        p.op("vector", lambda v: v.tensor_scalar(
            out=sqsc[:], in0=x1, scalar1=C(19), scalar2=128.0, op0=ALU.mult,
            op1=ALU.add))
        p.op("vector", lambda v: v.tensor_copy(out=iqt[:], in_=sqsc[:]))
        p.op("vector", lambda v: v.tensor_tensor(
            out=ln1t[:], in0=sqsc[:], in1=iqt[:], op=ALU.subtract))
        p.op("vector", lambda v: v.scalar_tensor_tensor(
            out=htile[:], in0=ln1t[:], scalar=0.5, in1=iqt[:], op0=ALU.is_ge,
            op1=ALU.add))
        p.op("vector", lambda v: v.tensor_scalar(
            out=xq, in0=htile[:], scalar1=-128.0, scalar2=None, op0=ALU.add),
            inc=vv)
        ev_xq = vv.n
        p.wait("scalar", vv, ev_xq)
        p.op("scalar", lambda sc: sc.dma_start(
            out=xqs[row0:row0 + 128, :], in_=xq), inc=sto, k=16)
        return ev_h2

    attn_stack = ExitStack()
    qf = attn_stack.enter_context(nc.sbuf_tensor("qf", [128, DC * S], BF16))
    kf = attn_stack.enter_context(nc.sbuf_tensor("kf", [128, DC * S], BF16))
    wrowbc = attn_stack.enter_context(nc.sbuf_tensor("wrowbc", [128, D], F32))
    vrow = attn_stack.enter_context(nc.sbuf_tensor("vrow", [1, D], F32))
    wf32 = attn_stack.enter_context(nc.sbuf_tensor("wf32", [128, 2 * D], F32))

    def wload(p, src):
        for dc in range(DC):
            p.op("gpsimd", lambda g, dc=dc, src=src: g.dma_start(
                out=wbuf[:, dc * D:(dc + 1) * D],
                in_=src[dc * 128:(dc + 1) * 128, :]), inc=dG, k=16)

    def mm_evac(p, items, emit_group, emit_evac, banks):
        nb = len(banks)
        evbase = vv.n
        mmbase = mm.n
        for i, it in enumerate(items):
            if i >= nb:
                p.wait("tensor", vv, evbase + (i - nb + 1))
            emit_group(p, it, banks[i % nb])
            assert mm.n == mmbase + i + 1
            p.wait("vector", mm, mm.n)
            emit_evac(p, it, banks[i % nb])
            assert vv.n == evbase + i + 1

    xf = attn_stack.enter_context(nc.sbuf_tensor("xfb", [128, DC * S], BF16))
    dGq = {}

    def emit_A0(p, s_i):
        for dc in range(DC):
            p.op("gpsimd", lambda g, dc=dc, s_i=s_i: g.dma_start(
                out=xf[:, dc * S:(dc + 1) * S],
                in_=x_fm[s_i, dc * 128:(dc + 1) * 128, :]), inc=dG, k=16)
        wload(p, wqT)
        dGq[s_i] = dG.n

    for s_i in range(NSEQ):
        if True:
            p = PB(nc, f"APRO_{s_i}")
            if s_i not in dGq:
                emit_A0(p, s_i)
            p.wait("gpsimd", dG, dGq[s_i])
            p.wait("tensor", dG, dGq[s_i])

            q_items = [(cb, ec) for cb in range(S // KBW) for ec in range(DC)]

            def q_group(p, it, bank):
                cb, ec = it
                for dc in range(DC):
                    p.op("tensor", lambda t, cb=cb, ec=ec, dc=dc, bank=bank:
                         t.matmul(bank[:, 0:KBW],
                                  wbuf[:, dc * D + ec * 128: dc * D + ec * 128 + 128],
                                  xf[:, dc * S + cb * KBW: dc * S + (cb + 1) * KBW],
                                  start=(dc == 0), stop=(dc == DC - 1)),
                         inc=(mm if dc == DC - 1 else None))

            def q_evac(p, it, bank):
                cb, ec = it
                p.op("vector", lambda v, cb=cb, ec=ec, bank=bank: v.tensor_scalar(
                    out=qf[:, ec * S + cb * KBW: ec * S + (cb + 1) * KBW],
                    in0=bank[:, 0:KBW], scalar1=cfg.QSCALE, scalar2=None,
                    op0=ALU.mult), inc=vv)

            mm_evac(p, q_items, q_group, q_evac, [ps_a, ps_b])

            p.wait("gpsimd", mm, mm.n)
            wload(p, wkT)
            p.wait("tensor", dG, dG.n)
            p.wait("tensor", vv, vv.n)

            def k_evac(p, it, bank):
                cb, ec = it
                p.op("vector", lambda v, cb=cb, ec=ec, bank=bank: v.tensor_copy(
                    out=kf[:, ec * S + cb * KBW: ec * S + (cb + 1) * KBW],
                    in_=bank[:, 0:KBW]), inc=vv)

            mm_evac(p, q_items, q_group, k_evac, [ps_a, ps_b])

            for dc in range(DC):
                p.op("vector", lambda v, dc=dc: v.reduce_sum(
                    out=colsb[:, 32 + dc:33 + dc], in_=xf[:, dc * S:(dc + 1) * S],
                    axis=AX.X), inc=(vv if dc == DC - 1 else None))
            p.wait("tensor", vv, vv.n)
            p.wait("sync", dS, dS.n)
            p.wait("sync", dSb, dSb.n)
            mvb = mm.n
            for dc in range(DC):
                sem = dS if dc % 2 == 0 else dSb
                if dc >= 2:
                    p.wait("sync", mm, mvb + NDH * (dc - 1))
                    p.wait("sync", sem, sem.n)
                p.op("sync", lambda s, dc=dc: s.dma_start(
                    out=wf32[:, (dc % 2) * D:(dc % 2 + 1) * D],
                    in_=wvT[dc * 128:(dc + 1) * 128, :]), inc=sem, k=16)
                p.wait("tensor", sem, sem.n)
                for h in range(NDH):
                    bank = ps_c if h == 0 else ps_d
                    p.op("tensor", lambda t, h=h, dc=dc, bank=bank: t.matmul(
                        bank[0:1, 0:DH], colsb[:, 32 + dc:33 + dc],
                        wf32[:, (dc % 2) * D + h * DH: (dc % 2) * D + (h + 1) * DH],
                        start=(dc == 0), stop=(dc == DC - 1)), inc=mm)
            p.wait("vector", mm, mm.n)
            for h in range(NDH):
                bank = ps_c if h == 0 else ps_d
                p.op("vector", lambda v, h=h, bank=bank: v.tensor_copy(
                    out=vrow[0:1, h * DH:(h + 1) * DH], in_=bank[0:1, 0:DH]),
                    inc=(vv if h == NDH - 1 else None))
            p.wait("sync", vv, vv.n)
            p.op("sync", lambda s: s.dma_start(out=vrow_d, in_=vrow[0:1, :]),
                 inc=dS, k=16)
            p.wait("sync", dS, dS.n)
            for dc in range(DC):
                p.op("sync", lambda s, dc=dc: s.dma_start(
                    out=colsb[:, 40 + dc:41 + dc],
                    in_=_ap(vrow_d.tensor, dc * 128, [[1, 128], [1, 1]])),
                    inc=dS, k=16)

        p.wait("sync", dS, dS.n)
        p.wait("sync", dSb, dSb.n)
        p.wait("tensor", dS, dS.n)
        p.wait("tensor", dSb, dSb.n)
        mvb = mm.n
        for dc in range(DC):
            sem = dS if dc % 2 == 0 else dSb
            if dc >= 2:
                p.wait("sync", mm, mvb + NDH * (dc - 1))
                p.wait("sync", sem, sem.n)
            p.op("sync", lambda s, dc=dc: s.dma_start(
                out=wf32[:, (dc % 2) * D:(dc % 2 + 1) * D],
                in_=wfcT[dc * 128:(dc + 1) * 128, :]), inc=sem, k=16)
            p.wait("tensor", sem, sem.n)
            for h in range(NDH):
                bank = ps_c if h == 0 else ps_d
                p.op("tensor", lambda t, h=h, dc=dc, bank=bank: t.matmul(
                    bank[0:1, 0:DH], colsb[:, 40 + dc:41 + dc],
                    wf32[:, (dc % 2) * D + h * DH: (dc % 2) * D + (h + 1) * DH],
                    start=(dc == 0), stop=(dc == DC - 1)), inc=mm)
        for h in range(NDH):
            bank = ps_c if h == 0 else ps_d
            p.wait("vector", mm, mm.n - (NDH - 1 - h))
            p.op("vector", lambda v, h=h, bank=bank: v.tensor_copy(
                out=rowbuf[0:1, 0:DH], in_=bank[0:1, 0:DH]), inc=vv)
            ev_row = vv.n
            p.wait("tensor", vv, ev_row)
            p.op("tensor", lambda t, h=h: t.matmul(
                ps_s[0][:, 0:DH], esel[:], rowbuf[:, 0:DH],
                start=True, stop=True), inc=mm)
            p.wait("vector", mm, mm.n)
            p.op("vector", lambda v, h=h: v.tensor_copy(
                out=wrowbc[:, h * DH:(h + 1) * DH], in_=ps_s[0][:, 0:DH]),
                inc=vv)
        p.run()

        a3s = ExitStack()
        Ea = a3s.enter_context
        disc = Ea(nc.sbuf_tensor(f"disc{s_i}", [128, KBW], F32))
        xres = Ea(nc.sbuf_tensor(f"xres{s_i}", [128, 2 * D], F32))
        htile = Ea(nc.sbuf_tensor(f"htile{s_i}", [128, D], F32))
        atile = Ea(nc.sbuf_tensor(f"atile{s_i}", [128, D], F32))
        ln1t = Ea(nc.sbuf_tensor(f"ln1t{s_i}", [128, D], F32))
        sqsc = Ea(nc.sbuf_tensor(f"sqsc{s_i}", [128, D], F32))
        x1t = Ea(nc.sbuf_tensor(f"x1t{s_i}", [128, 2 * D], F32))
        iqt = Ea(nc.sbuf_tensor(f"iqt{s_i}", [128, D], I32))
        iqtb = Ea(nc.sbuf_tensor(f"iqtb{s_i}", [128, D], I32))
        xqt = Ea(nc.sbuf_tensor(f"xqt{s_i}", [128, 2 * D], BF16))
        eb = (xres, htile, atile, ln1t, sqsc, x1t, iqt, iqtb, xqt)
        if True:
            if True:
                p = PB(nc, f"A3_{s_i}")
                if s_i + 1 < NSEQ:
                    emit_A0(p, s_i + 1)
                dO_bases = {0: dO.n, 1: dOb.n}
                exp_ev = {}
                mm_at = {}
                epi_q = {}
                for qt in range(NGRP * NQT):
                    row0 = s_i * S + qt * 128
                    qcol = qt * 128
                    ldsem = dX if qt % 2 == 0 else dXb
                    if qt >= 2:
                        p.wait("sync", vv, ev_h2_list[qt - 2])
                    p.op("sync", lambda s, row0=row0, qt=qt: s.dma_start(
                        out=xres[:, (qt % 2) * D:(qt % 2 + 1) * D],
                        in_=x_tm[row0:row0 + 128, :]), inc=ldsem, k=16)
                    if qt > 0:
                        p.wait("tensor", ssm, exp_ev[qt - 1])
                    for kb in range(KB):
                        for dc in range(DC):
                            p.op("tensor", lambda t, qcol=qcol, kb=kb, dc=dc:
                                 t.matmul(
                                     ps_s[kb][:, 0:KBW],
                                     qf[:, dc * S + qcol: dc * S + qcol + 128],
                                     kf[:, dc * S + kb * KBW: dc * S + (kb + 1) * KBW],
                                     start=(dc == 0), stop=(dc == DC - 1)),
                                 inc=(mm if kb == KB - 1 and dc == DC - 1 else None))
                    mm_at[qt] = mm.n
                    p.wait("vector", mm, mm.n)
                    for kb in range(KB):
                        p.op("vector", lambda v, kb=kb: v.reduce_max(
                            out=colsb[:, kb:kb + 1], in_=ps_s[kb][:, 0:KBW],
                            axis=AX.X, negate=True))
                    p.op("vector", lambda v: v.tensor_reduce(
                        out=colsb[:, 8:9], in_=colsb[:, 0:KB], axis=AX.X,
                        op=ALU.min), inc=vv)
                    ev_negm = vv.n
                    p.wait("scalar", vv, ev_negm)
                    for kb in range(KB):
                        p.op("scalar", lambda sc, kb=kb: sc.activation(
                            out=disc[:], in_=ps_s[kb][:, 0:KBW], func=AF.Exp,
                            bias=colsb[:, 8:9], scale=1.0,
                            accum_out=colsb[:, 16 + kb:17 + kb]),
                            inc=(ssm if kb == KB - 1 else None))
                    exp_ev[qt] = ssm.n
                    if qt >= 1:
                        row0p, parp, stop, depp, ldp = epi_q[qt - 1]
                        ev = emit_epilogue(p, row0p, None, None, mm_at[qt - 1],
                                           eb, "attn", parp, dep_store=depp,
                                           sto=stop, ldsem=ldp)
                        ev_h2_list.append(ev)
                    p.wait("vector", ssm, exp_ev[qt])
                    p.op("vector", lambda v: v.reduce_sum(
                        out=colsb[:, 24:25], in_=colsb[:, 16:16 + KB], axis=AX.X))
                    p.op("vector", lambda v: v.reciprocal(
                        out=colsb[:, 25:26], in_=colsb[:, 24:25]))
                    p.op("vector", lambda v: v.tensor_scalar(
                        out=colsb[:, 26:27], in0=colsb[:, 24:25],
                        scalar1=colsb[:, 25:26], scalar2=None, op0=ALU.mult))
                    p.op("vector", lambda v: v.tensor_scalar(
                        out=atile[:], in0=wrowbc[:], scalar1=colsb[:, 26:27],
                        scalar2=None, op0=ALU.mult), inc=vv)
                    par = qt % 2
                    sto = dO if par == 0 else dOb
                    dep = dO_bases[par] + (48 * (qt // 2) if qt >= 2 else 0)
                    if qt == 0:
                        ev_h2_list = []
                    epi_q[qt] = (row0, par, sto, dep, ldsem)
                qt_last = NGRP * NQT - 1
                row0p, parp, stop, depp, ldp = epi_q[qt_last]
                ev = emit_epilogue(p, row0p, None, None, mm_at[qt_last], eb,
                                   "attn", parp, dep_store=depp, sto=stop,
                                   ldsem=ldp)
                ev_h2_list.append(ev)
                p.run()
        p = PB(nc, f"A3drain_{s_i}")
        for eng in ("sync", "scalar", "vector", "tensor", "gpsimd"):
            p.wait(eng, dO, dO.n)
            p.wait(eng, dOb, dOb.n)
            p.wait(eng, dX, dX.n)
            p.wait(eng, dXb, dXb.n)
        p.run()
        a3s.close()

    attn_stack.close()

    tw = stack.enter_context(nc.sbuf_tensor("tw", [128, DC * H], BF16))

    def quantize_weights(srcT, nch, width, dst_off):
        with nc.sbuf_tensor(f"qstage_{dst_off}", [128, width], F32) as qstage, \
             nc.sbuf_tensor(f"qstage2_{dst_off}", [128, width], F32) as qstage2, \
             nc.sbuf_tensor(f"qz_{dst_off}", [128, width], F32) as qz, \
             nc.sbuf_tensor(f"qa_{dst_off}", [128, width], F32) as qa:
            p = PB(nc, "W1")
            p.wait("sync", dS, dS.n)
            p.wait("sync", dSb, dSb.n)
            if debug:
                for eng in ("sync", "vector"):
                    p.wait(eng, dOb, dOb.n)
                    p.wait(eng, dO, dO.n)
            scol = wqcols[:, 0:1] if dst_off == 0 else wqcols[:, 1:2]
            for c in range(nch):
                stage = qstage if c % 2 == 0 else qstage2
                sem = dS if c % 2 == 0 else dSb
                if c >= 2:
                    p.wait("sync", vv, vv.n - 3)
                p.op("sync", lambda s, c=c, stage=stage: s.dma_start(
                    out=stage[:, 0:width], in_=srcT[c * 128:(c + 1) * 128, :]),
                    inc=sem, k=16)
                p.wait("vector", sem, sem.n)
                p.op("vector", lambda v, stage=stage: v.tensor_scalar(
                    out=qz[:, 0:width], in0=stage[:, 0:width], scalar1=scol,
                    scalar2=None, op0=ALU.mult), inc=vv)
                p.op("vector", lambda v: v.tensor_scalar(
                    out=qa[:, 0:width], in0=qz[:, 0:width], scalar1=0.5,
                    scalar2=None, op0=ALU.is_ge), inc=vv)
                p.op("vector", lambda v, c=c: v.scalar_tensor_tensor(
                    out=tw[:, c * width:(c + 1) * width], in0=qz[:, 0:width],
                    scalar=-0.5, in1=qa[:, 0:width], op0=ALU.is_le,
                    op1=ALU.subtract), inc=vv)
            p.run()

    quantize_weights(w1T, DC, H, 0)

    T1W2 = T1 + 2
    NSB1 = T1 // 128
    NQG = min(8, HC)
    WW = T1 + 32
    with nc.sbuf_tensor("xqF", [128, 2 * DC * WW], BF16) as xqF, \
         nc.sbuf_tensor("ubuf", [128, HC * T1W2], F32) as ubuf, \
         nc.sbuf_tensor("vt1", [128, 4 * T1W2], F32) as vt1, \
         nc.sbuf_tensor("vt2", [128, 2 * T1W2], F32) as vt2, \
         nc.sbuf_tensor("maxM", [128, T1], F32) as maxM, \
         nc.sbuf_tensor("rbuf", [128, NQG * T1], BF16) as rbuf, \
         nc.sbuf_tensor("f1bc", [128, 4 * T1W2], F32) as f1bc, \
         nc.sbuf_tensor("s2bc8", [128, NQG * T1], F32) as s2bc8, \
         nc.sbuf_tensor("zscr", [128, NQG * T1], F32) as zscr, \
         nc.sbuf_tensor("iqq", [128, NQG * T1], I32) as iqq, \
         nc.sbuf_tensor("dsb", [128, NQG * T1], BF16) as dsb, \
         nc.sbuf_tensor("accb", [128, NQG * T1], BF16) as accb, \
         nc.sbuf_tensor("hqst", [128, HC * T1], BF16) as hqst:
        p = PB(nc, "F1")
        p.wait("sync", dO, dO.n)
        p.wait("sync", dOb, dOb.n)
        p.wait("sync", dX, dX.n)
        p.wait("sync", dXb, dXb.n)
        p.wait("sync", dS, dS.n)

        def tgeom(i):
            s_i = (i * T1) // S
            tb = i * T1
            lo = max(tb - 1, s_i * S)
            hi = min(tb + T1 + 1, (s_i + 1) * S)
            return tb, lo, hi, hi - lo, lo - (tb - 1), min(max(tb - 16, 0), T - WW)

        ldsems = (dX, dXb)
        stsems = (dO, dOb)
        mm_after_u = {}
        lad = {}
        ev_u = {}
        ev_conv = {}
        ss_vt1 = {}
        ss_silu = {}
        st_cnt = {}
        ld_cnt = {}
        s2_ld = {}
        lastA_ev = {}
        s2sems = (sA, sB)

        def emit_loads(i):
            tb, lo, hi, W, off, w0 = tgeom(i)
            par = i % 2
            sem = ldsems[par]
            if i >= 2:
                p.wait("sync", mm, mm_after_u[i - 2])
                p.wait("sync", sem, sem.n)
            for dc in range(DC):
                p.op("sync", lambda s, dc=dc, w0=w0, par=par: s.dma_start(
                    out=xqF[:, (par * DC + dc) * WW:(par * DC + dc + 1) * WW],
                    in_=xqs[w0:w0 + WW, dc * 128:(dc + 1) * 128], transpose=True),
                    inc=sem, k=16)
            p.op("sync", lambda s, lo=lo, W=W, par=par: s.dma_start(
                out=f1bc[:, 2 * par * T1W2:2 * par * T1W2 + W],
                in_=_ap(f1s.tensor, lo, [[0, 128], [1, W]])), inc=sem, k=16)
            ld_cnt[i] = sem.n

        def emit_conv_tail(i, hc):
            cq = hc % 4
            cpar = hc % 2
            p.wait("vector", ssm, ss_vt1[(i, hc)])
            p.op("vector", lambda v, hc=hc, cq=cq, cpar=cpar:
                 v.scalar_tensor_tensor(
                out=vt2[:, cpar * T1W2:cpar * T1W2 + T1],
                in0=ubuf[:, hc * T1W2:hc * T1W2 + T1],
                scalar=cwcol[:, 3 * hc:3 * hc + 1],
                in1=vt1[:, cq * T1W2:cq * T1W2 + T1],
                op0=ALU.mult, op1=ALU.add))
            p.op("vector", lambda v, hc=hc, cq=cq, cpar=cpar:
                 v.scalar_tensor_tensor(
                out=vt1[:, cq * T1W2:cq * T1W2 + T1],
                in0=ubuf[:, hc * T1W2 + 2:hc * T1W2 + 2 + T1],
                scalar=cwcol[:, 3 * hc + 2:3 * hc + 3],
                in1=vt2[:, cpar * T1W2:cpar * T1W2 + T1],
                op0=ALU.mult, op1=ALU.add), inc=vv)
            ev_conv[(i, hc)] = vv.n
            p.wait("scalar", vv, ev_conv[(i, hc)])
            p.op("scalar", lambda sc, hc=hc, cq=cq: sc.activation(
                out=ubuf[:, hc * T1W2 + 1:hc * T1W2 + 1 + T1],
                in_=vt1[:, cq * T1W2:cq * T1W2 + T1], func=AF.Silu),
                inc=ssm)
            ss_silu[(i, hc)] = ssm.n

        qg_iqq = {}

        def emit_quant_A(i, g0):
            ng = min(NQG, HC - g0)
            FD = ng * T1
            s28 = s2bc8[:, 0:FD]
            if g0 == 0:
                p.wait("vector", s2sems[i % 2], s2_ld[i])
            p.op("vector", lambda v, g0=g0, ng=ng, FD=FD, s28=s28: v.tensor_tensor(
                out=zscr[:, 0:FD],
                in0=_ap(ubuf, g0 * T1W2 + 1,
                        [[HC * T1W2, 128], [T1W2, ng], [1, T1]]),
                in1=s28, op=ALU.mult), inc=vv)
            ev_z = vv.n
            lastA_ev[i] = ev_z
            p.wait("scalar", vv, ev_z)
            if g0 // NQG >= 1:
                p.wait("scalar", vv, lad[(i, g0 // NQG - 1)])
            elif i >= 1:
                p.wait("scalar", vv, lad[(i - 1, (HC - 1) // NQG)])
            p.op("scalar", lambda sc, FD=FD: sc.activation(
                out=iqq[:, 0:FD], in_=zscr[:, 0:FD], func=AF.Identity,
                bias=c128[:, 0:1], scale=1.0), inc=ssm)
            qg_iqq[(i, g0 // NQG)] = ssm.n

        def emit_quant_B(i, g0):
            ng = min(NQG, HC - g0)
            FD = ng * T1
            if g0 == 0:
                if i >= 1:
                    p.wait("vector", stsems[(i - 1) % 2], st_cnt[i - 1])
            p.wait("vector", ssm, qg_iqq[(i, g0 // NQG)])
            p.op("vector", lambda v, FD=FD: v.scalar_tensor_tensor(
                out=dsb[:, 0:FD], in0=zscr[:, 0:FD], scalar=128.0,
                in1=iqq[:, 0:FD], op0=ALU.add, op1=ALU.subtract))
            p.op("vector", lambda v, FD=FD: v.tensor_scalar(
                out=accb[:, 0:FD], in0=dsb[:, 0:FD], scalar1=-1.5,
                scalar2=-130.0, op0=ALU.is_ge, op1=ALU.add))
            for thr in (-0.5, 0.5, 1.5, 2.5):
                p.op("vector", lambda v, FD=FD, thr=thr: v.tensor_scalar(
                    out=rbuf[:, 0:FD], in0=dsb[:, 0:FD], scalar1=thr,
                    scalar2=None, op0=ALU.is_ge))
                p.op("vector", lambda v, FD=FD: v.tensor_tensor(
                    out=accb[:, 0:FD], in0=accb[:, 0:FD], in1=rbuf[:, 0:FD],
                    op=ALU.add))
            p.op("vector", lambda v, g0=g0, FD=FD: v.tensor_tensor(
                out=hqst[:, g0 * T1:g0 * T1 + FD], in0=accb[:, 0:FD],
                in1=iqq[:, 0:FD], op=ALU.add), inc=vv)
            lad[(i, g0 // NQG)] = vv.n

        def emit_store(i):
            par = i % 2
            sem = stsems[par]
            p.wait("sync", vv, lad[(i, (HC - 1) // NQG)])
            p.wait("sync", sem, sem.n)
            p.op("sync", lambda s, i=i: s.dma_start(
                out=_ap(hqs.tensor, i * HC * 128 * T1,
                        [[T1, 128], [128 * T1, HC], [1, T1]]),
                in_=_ap(hqst, 0, [[HC * T1, 128], [T1, HC], [1, T1]])),
                inc=sem, k=16)
            st_cnt[i] = sem.n

        emit_loads(0)
        if NT1 > 1:
            emit_loads(1)
        for i in range(NT1):
            tb, lo, hi, W, off, w0 = tgeom(i)
            par = i % 2
            fb_raw = 2 * par * T1W2
            fb_s = fb_raw + T1W2
            p.wait("vector", ldsems[par], ld_cnt[i])
            p.op("vector", lambda v, fb_s=fb_s: v.memset(
                f1bc[:, fb_s:fb_s + T1W2], 0.0))
            p.op("vector", lambda v, fb_raw=fb_raw, fb_s=fb_s, off=off, W=W:
                 v.tensor_scalar(
                     out=f1bc[:, fb_s + off:fb_s + off + W],
                     in0=f1bc[:, fb_raw:fb_raw + W],
                     scalar1=wqcols[:, 2:3], scalar2=None, op0=ALU.mult), inc=vv)
            ev_f1bc = vv.n
            p.wait("tensor", vv, ev_f1bc)
            p.wait("tensor", ldsems[par], ld_cnt[i])
            for hc in range(HC):
                bank = (ps_a, ps_b, ps_s[0], ps_s[1])[hc % 4]
                cpar = hc % 2
                if i >= 1 and hc % NQG == 0:
                    emit_quant_A(i - 1, hc)
                if i >= 1 and hc % NQG == NQG // 2:
                    emit_quant_B(i - 1, hc - NQG // 2)
                if hc >= 4:
                    p.wait("tensor", vv, ev_u[(i, hc - 4)])
                elif i >= 1:
                    p.wait("tensor", vv, ev_u[(i - 1, HC - 4 + hc)])
                for dc in range(DC):
                    p.op("tensor", lambda t, hc=hc, dc=dc, bank=bank, par=par,
                         lo=lo, hi=hi, w0=w0, W=W: t.matmul(
                        bank[:, 0:W],
                        tw[:, dc * H + hc * 128: dc * H + hc * 128 + 128],
                        xqF[:, (par * DC + dc) * WW + (lo - w0):
                            (par * DC + dc) * WW + (hi - w0)],
                        start=(dc == 0), stop=(dc == DC - 1)),
                        inc=(mm if dc == DC - 1 else None))
                if hc == HC - 1:
                    mm_after_u[i] = mm.n
                p.wait("vector", mm, mm.n)
                p.op("vector", lambda v, hc=hc, bank=bank, off=off, W=W,
                     fb_s=fb_s: v.tensor_tensor(
                    out=ubuf[:, hc * T1W2 + off: hc * T1W2 + off + W],
                    in0=bank[:, 0:W],
                    in1=f1bc[:, fb_s + off:fb_s + off + W], op=ALU.mult),
                    inc=vv)
                ev_u[(i, hc)] = vv.n
                if off == 1:
                    p.op("vector", lambda v, hc=hc: v.memset(
                        ubuf[:, hc * T1W2:hc * T1W2 + 1], 0.0))
                if off + W < T1W2:
                    p.op("vector", lambda v, hc=hc: v.memset(
                        ubuf[:, hc * T1W2 + off + W:hc * T1W2 + T1W2], 0.0))
                p.wait("scalar", vv, ev_u[(i, hc)])
                p.op("scalar", lambda sc, hc=hc: sc.activation(
                    out=vt1[:, (hc % 4) * T1W2:(hc % 4) * T1W2 + T1],
                    in_=ubuf[:, hc * T1W2 + 1:hc * T1W2 + 1 + T1],
                    func=AF.Identity,
                    scale=cwcol[:, 3 * hc + 1:3 * hc + 2],
                    bias=cbcol[:, hc:hc + 1]), inc=ssm)
                ss_vt1[(i, hc)] = ssm.n
                if hc >= 3:
                    emit_conv_tail(i, hc - 3)
            for tl in range(HC - 3, HC):
                emit_conv_tail(i, tl)
            if i >= 1:
                emit_store(i - 1)
            if i + 2 < NT1:
                emit_loads(i + 2)
            p.wait("vector", ssm, ss_silu[(i, HC - 1)])
            p.op("vector", lambda v: v.reduce_max(
                out=maxM[:, 0:T1],
                in_=_ap(ubuf, 1, [[HC * T1W2, 128], [1, T1], [T1W2, HC]]),
                axis=AX.X, apply_absolute_value=True), inc=vv)
            ev_mabs = vv.n
            p.wait("tensor", vv, ev_mabs)
            for j in range(NSB1):
                p.op("tensor", lambda t, j=j: t.matmul(
                    ps_c[:, j * 128:(j + 1) * 128],
                    maxM[:, j * 128:(j + 1) * 128], identf[:],
                    start=True, stop=True), inc=(mm if j == NSB1 - 1 else None))
            p.wait("vector", mm, mm.n)
            for j in range(NSB1):
                p.op("vector", lambda v, j=j: v.reduce_max(
                    out=cols[:, 0:1], in_=ps_c[:, j * 128:(j + 1) * 128],
                    axis=AX.X))
                p.op("vector", lambda v, j=j: v.tensor_scalar(
                    out=cols[:, 1:2], in0=cols[:, 0:1], scalar1=1e-5,
                    scalar2=None, op0=ALU.max))
                p.op("vector", lambda v, j=j: v.reciprocal(
                    out=cols[:, 2:3], in_=cols[:, 1:2]))
                p.op("vector", lambda v, j=j: v.tensor_scalar(
                    out=cols[:, 3 + j:4 + j], in0=cols[:, 2:3], scalar1=127.0,
                    scalar2=None, op0=ALU.mult))
                p.op("vector", lambda v, j=j: v.reciprocal(
                    out=cols[:, 8:9], in_=cols[:, 3 + j:4 + j]))
                p.op("vector", lambda v, i=i, j=j: v.tensor_scalar(
                    out=f2cols[:, i * NSB1 + j:i * NSB1 + j + 1],
                    in0=cols[:, 8:9], scalar1=wqcols[:, 3:4], scalar2=None,
                    op0=ALU.mult), inc=vv)
            ev_scol = vv.n
            p.wait("tensor", vv, ev_scol)
            for j in range(NSB1):
                p.op("tensor", lambda t, j=j: t.matmul(
                    ps_d[0:1, j * 128:(j + 1) * 128],
                    cols[:, 3 + j:4 + j], identf[:], start=True, stop=True),
                    inc=(mm if j == NSB1 - 1 else None))
            p.wait("vector", mm, mm.n)
            if i >= 1:
                p.wait("vector", s2sems[(i - 1) % 2], s2_ld[i - 1])
            p.op("vector", lambda v: v.tensor_copy(
                out=rowbuf[0:1, 0:T1], in_=ps_d[0:1, 0:T1]), inc=vv)
            ev_row = vv.n
            sem = s2sems[i % 2]
            if i >= 2:
                p.wait("gpsimd", sem, s2_ld[i - 2])
            p.wait("gpsimd", vv, ev_row)
            p.op("gpsimd", lambda g, i=i: g.dma_start(
                out=s2row[i % 2:i % 2 + 1, 0:T1], in_=rowbuf[0:1, 0:T1]),
                inc=sem, k=16)
            p.wait("gpsimd", sem, sem.n)
            if i >= 1:
                p.wait("gpsimd", vv, lastA_ev[i - 1])
            p.op("gpsimd", lambda g, i=i: g.dma_start(
                out=_ap(s2bc8, 0, [[NQG * T1, 128], [T1, NQG], [1, T1]]),
                in_=_ap(s2row.tensor, (i % 2) * T1,
                        [[0, 128], [0, NQG], [1, T1]])), inc=sem, k=16)
            s2_ld[i] = sem.n
        for g0 in range(0, HC, NQG):
            emit_quant_A(NT1 - 1, g0)
            emit_quant_B(NT1 - 1, g0)
        emit_store(NT1 - 1)
        p.run()

    quantize_weights(w2T, HC, D, 1)

    NSB1F = T1 // 128
    NHQS = 3
    with nc.sbuf_tensor("hq2", [128, NHQS * HC * T1], BF16) as hq2, \
         nc.sbuf_tensor("xres2", [128, 2 * D], F32) as xres2, \
         nc.sbuf_tensor("htile2", [128, D], F32) as htile2, \
         nc.sbuf_tensor("atile2", [128, D], F32) as atile2, \
         nc.sbuf_tensor("ln1t2", [128, D], F32) as ln1t2, \
         nc.sbuf_tensor("sqsc2", [128, D], F32) as sqsc2, \
         nc.sbuf_tensor("x1t2", [128, 2 * D], F32) as x1t2, \
         nc.sbuf_tensor("iqt2", [128, D], I32) as iqt2, \
         nc.sbuf_tensor("iqt2b", [128, D], I32) as iqt2b, \
         nc.sbuf_tensor("xqt2", [128, 2 * D], BF16) as xqt2:
        eb2 = (xres2, htile2, atile2, ln1t2, sqsc2, x1t2, iqt2, iqt2b, xqt2)
        p = PB(nc, "F2")
        p.wait("sync", dS, dS.n)
        p.wait("sync", dSb, dSb.n)
        p.wait("sync", dO, dO.n)
        p.wait("sync", dOb, dOb.n)
        p.wait("sync", dX, dX.n)
        p.wait("sync", dXb, dXb.n)
        hq_ld = {}
        mm_after_f2 = {}
        st_cnt2 = {}
        dO_bases2 = {0: dO.n, 1: dOb.n}
        ev_h1_list = []

        def load_hq(i):
            if i >= NHQS:
                p.wait("sync", mm, mm_after_f2[i - NHQS])
            sem = dS if i % 2 == 0 else dSb
            p.wait("sync", sem, sem.n)
            p.op("sync", lambda s, i=i: s.dma_start(
                out=_ap(hq2, (i % NHQS) * HC * T1,
                        [[NHQS * HC * T1, 128], [T1, HC], [1, T1]]),
                in_=_ap(hqs.tensor, i * HC * 128 * T1,
                        [[T1, 128], [128 * T1, HC], [1, T1]])), inc=sem, k=16)
            hq_ld[i] = (sem, sem.n)

        for i in range(min(NHQS, NT1)):
            load_hq(i)
        for i in range(NT1):
            slot = (i % NHQS) * HC * T1
            for sq in range(NSB1F):
                st = i * NSB1F + sq
                row0 = i * T1 + sq * 128
                ldsem = dX if st % 2 == 0 else dXb
                if st >= 2:
                    p.wait("sync", vv, ev_h1_list[st - 2])
                p.op("sync", lambda s, row0=row0, st=st: s.dma_start(
                    out=xres2[:, (st % 2) * D:(st % 2 + 1) * D],
                    in_=x1s[row0:row0 + 128, :]), inc=ldsem, k=16)
                if sq == 0:
                    sem, cnt = hq_ld[i]
                    p.wait("tensor", sem, cnt)
                if st >= 2:
                    p.wait("tensor", vv, ev_h1_list[st - 2])
                banks = (ps_c, ps_d) if st % 2 == 0 else (ps_s[0], ps_s[1])
                for hc in range(HC):
                    last = (hc == HC - 1)
                    p.op("tensor", lambda t, slot=slot, sq=sq, hc=hc,
                         banks=banks: t.matmul(
                        banks[0][:, 0:DH],
                        hq2[:, slot + hc * T1 + sq * 128:
                            slot + hc * T1 + sq * 128 + 128],
                        tw[:, hc * D + 0:hc * D + DH],
                        start=(hc == 0), stop=(hc == HC - 1)),
                        inc=(mm if (last and NDH == 1) else None))
                    if NDH > 1:
                        p.op("tensor", lambda t, slot=slot, sq=sq, hc=hc,
                             banks=banks: t.matmul(
                            banks[1][:, 0:DH],
                            hq2[:, slot + hc * T1 + sq * 128:
                                slot + hc * T1 + sq * 128 + 128],
                            tw[:, hc * D + DH:hc * D + 2 * DH],
                            start=(hc == 0), stop=(hc == HC - 1)),
                            inc=(mm if last else None))
                par = st % 2
                sto = dO if par == 0 else dOb
                dep = st_cnt2.get(st - 2, dO_bases2[par])
                ev = emit_epilogue(p, row0, banks[0], banks[1], mm.n, eb2,
                                   "ffn", par, dep_store=dep, sto=sto,
                                   ldsem=ldsem)
                ev_h1_list.append(ev)
                st_cnt2[st] = sto.n
            mm_after_f2[i] = mm.n
            if i + NHQS < NT1:
                load_hq(i + NHQS)
        p.run()

    p = PB(nc, "END")
    p.wait("sync", dS, dS.n)
    p.wait("sync", dSb, dSb.n)
    p.wait("sync", dX, dX.n)
    p.wait("sync", dXb, dXb.n)
    p.wait("scalar", dO, dO.n)
    p.wait("scalar", dOb, dOb.n)
    p.wait("gpsimd", dG, dG.n)
    p.wait("gpsimd", sA, sA.n)
    p.wait("gpsimd", sB, sB.n)
    p.run()

    stack.close()
    return nc


def host_inputs(cfg, core, x, w_qkv, w_fc, b_fc, gamma, beta, w1, conv_w,
                conv_b, w2):
    S, D, NSEQ = cfg.S, cfg.D, cfg.NSEQ
    b0 = core * NSEQ
    xs = x[b0:b0 + NSEQ]
    return {
        "x_tm": np.ascontiguousarray(xs.reshape(NSEQ * S, D)),
        "x_fm": np.ascontiguousarray(xs.transpose(0, 2, 1)),
        "wqT": np.ascontiguousarray(w_qkv[0:D].T),
        "wkT": np.ascontiguousarray(w_qkv[D:2 * D].T),
        "wvT": np.ascontiguousarray(w_qkv[2 * D:3 * D].T),
        "wfcT": np.ascontiguousarray(w_fc.T),
        "bfc": np.ascontiguousarray(b_fc.reshape(1, D)),
        "gamma": np.ascontiguousarray(gamma.reshape(1, D)),
        "beta": np.ascontiguousarray(beta.reshape(1, D)),
        "w1T": np.ascontiguousarray(w1.T),
        "w2T": np.ascontiguousarray(w2.T),
        "cw": np.ascontiguousarray(conv_w.reshape(cfg.H, 3)),
        "cb": np.ascontiguousarray(conv_b.reshape(cfg.H, 1)),
    }


def ref_forward(cfg, x, w_qkv, w_fc, b_fc, gamma, beta, w1, conv_w, conv_b,
                w2, dtype=np.float64):
    S, D, H = cfg.S, cfg.D, cfg.H
    f = lambda a: np.asarray(a, dtype=dtype)
    x, w_qkv, w_fc, b_fc = f(x), f(w_qkv), f(w_fc), f(b_fc)
    gamma, beta, w1, conv_w, conv_b, w2 = (f(gamma), f(beta), f(w1),
                                           f(conv_w), f(conv_b), f(w2))

    def ln(h):
        mu = h.mean(-1, keepdims=True)
        va = h.var(-1, keepdims=True)
        return (h - mu) / np.sqrt(va + cfg.LN_EPS)

    qkv = np.einsum("bsd,ed->bse", x, w_qkv)
    q, k, v = np.split(qkv, 3, axis=-1)
    s = np.einsum("bqd,bkd->bqk", q, k) * cfg.QSCALE
    s = s - s.max(-1, keepdims=True)
    a = np.exp(s)
    a /= a.sum(-1, keepdims=True)
    y = a.sum(-1, keepdims=True) * v.sum(1, keepdims=True)
    y = np.einsum("bqd,ed->bqe", y, w_fc) + b_fc
    h = x + y
    attn = ln(h) * gamma + beta
    x1 = ln(x + attn)

    def act_quant(t):
        sc = 127.0 / np.clip(np.abs(t).max(-1, keepdims=True), 1e-5, None)
        q = np.clip(np.round(t * sc), -128, 127)
        return q / sc

    def weight_quant(w):
        sc = 1.0 / np.clip(np.abs(w).mean(), 1e-5, None)
        return np.clip(np.round(w * sc), -1, 1) / sc

    hq = np.einsum("bsd,hd->bsh", act_quant(x1), weight_quant(w1))
    cwk = conv_w.reshape(H, 3)
    hp = np.pad(hq, ((0, 0), (1, 1), (0, 0)))
    conv = (hp[:, :-2, :] * cwk[:, 0] + hp[:, 1:-1, :] * cwk[:, 1] +
            hp[:, 2:, :] * cwk[:, 2]) + conv_b.reshape(H)
    sl = conv / (1.0 + np.exp(-conv))
    out = np.einsum("bsh,eh->bse", act_quant(sl), weight_quant(w2))
    x2 = ln(x1 + out)
    return x2


from concourse.bass_utils import run_bass_kernel_spmd

_N_CORES = 8


def kernel(L0_attn_x, L0_attn_w_qkv, L0_attn_w_fc, L0_attn_b_fc, L0_attn_gamma,
           L0_attn_beta, L0_ffn_w1, L0_ffn_conv_w, L0_ffn_conv_b, L0_ffn_w2):
    cfg = Cfg(S=2048, D=1024, H=4096, NSEQ=2, T1=256, T2=512)
    x = np.asarray(L0_attn_x)
    nc = build_program(cfg)
    in_maps = [
        host_inputs(cfg, c, x, np.asarray(L0_attn_w_qkv),
                    np.asarray(L0_attn_w_fc), np.asarray(L0_attn_b_fc),
                    np.asarray(L0_attn_gamma), np.asarray(L0_attn_beta),
                    np.asarray(L0_ffn_w1), np.asarray(L0_ffn_conv_w),
                    np.asarray(L0_ffn_conv_b), np.asarray(L0_ffn_w2))
        for c in range(_N_CORES)
    ]
    res = run_bass_kernel_spmd(nc, in_maps, core_ids=list(range(_N_CORES)))
    parts = [np.asarray(r["y"]).reshape(cfg.NSEQ, cfg.S, cfg.D)
             for r in res.results]
    return np.ascontiguousarray(np.concatenate(parts, axis=0), dtype=np.float32)


# version 20 (final)
# speedup vs baseline: 1.0355x; 1.0355x over previous
import numpy as np
import ml_dtypes
from contextlib import ExitStack

import concourse.bass as bass
import concourse.mybir as mybir

F32 = mybir.dt.float32
BF16 = mybir.dt.bfloat16
I32 = mybir.dt.int32
AF = mybir.ActivationFunctionType
ALU = mybir.AluOpType
AX = mybir.AxisListType


class Cfg:
    def __init__(self, S=2048, D=1024, H=4096, NSEQ=2, T1=256, T2=512):
        self.S, self.D, self.H, self.NSEQ = S, D, H, NSEQ
        self.DC, self.HC = D // 128, H // 128
        self.T1, self.T2 = T1, T2
        self.G = min(512, S)
        self.KBW = min(512, S)
        self.KB = S // self.KBW
        self.T = NSEQ * S
        self.ST = S // 128
        self.NGRP = S // self.G
        self.NQT = self.G // 128
        self.LN_EPS = 1e-5
        self.QSCALE = float(D) ** (-0.5)


class Sem:
    def __init__(self, h):
        self.h, self.n = h, 0


class PB:
    ENGS = ("sync", "scalar", "vector", "tensor", "gpsimd")

    def __init__(self, nc, name=""):
        self.nc, self.name = nc, name
        self.items = {e: [] for e in self.ENGS}

    def op(self, eng, fn, inc=None, k=1, drain=None):
        if drain is None:
            drain = eng in ("scalar", "vector", "gpsimd")
        tgt = None
        if inc is not None:
            inc.n += k
            tgt = (inc.h, k)

        def cl(e, fn=fn, tgt=tgt, drain=drain):
            ins = fn(e)
            if tgt is not None:
                ins.then_inc(*tgt)
            if drain:
                e.drain()

        self.items[eng].append(cl)

    def wait(self, eng, sem, val):
        self.items[eng].append(lambda e, h=sem.h, v=val: e.wait_ge(h, v))

    def run(self):
        with self.nc.Block() as block:
            for name in self.ENGS:
                fns = self.items[name]
                if not fns:
                    continue

                def body(e, fns=fns):
                    for f in fns:
                        f(e)

                getattr(block, name)(body)


def _ap(handle, offset, dims):
    return bass.AP(handle, offset, dims)


def build_program(cfg: Cfg, debug=False):
    nc = bass.Bass("TRN2", target_bir_lowering=False)
    S, D, H, DC, HC = cfg.S, cfg.D, cfg.H, cfg.DC, cfg.HC
    T, G, KB, KBW = cfg.T, cfg.G, cfg.KB, cfg.KBW
    T1, T2, ST = cfg.T1, cfg.T2, cfg.ST
    NGRP, NQT, NSEQ = cfg.NGRP, cfg.NQT, cfg.NSEQ
    NT1, NT2 = T // T1, T // T2
    EPS = cfg.LN_EPS
    DH = min(512, D)
    NDH = D // DH
    NPTB = (ST + 3) // 4

    def din(name, shape):
        return nc.dram_tensor(name, list(shape), F32, kind="ExternalInput").ap()

    x_tm = din("x_tm", (T, D))
    x_fm = din("x_fm", (NSEQ, D, S))
    wqT = din("wqT", (D, D))
    wkT = din("wkT", (D, D))
    wvT = din("wvT", (D, D))
    wfcT = din("wfcT", (D, D))
    bfc_d = din("bfc", (1, D))
    gamma_d = din("gamma", (1, D))
    beta_d = din("beta", (1, D))
    w1T = din("w1T", (D, H))
    w2T = din("w2T", (H, D))
    cw_d = din("cw", (H, 3))
    cb_d = din("cb", (H, 1))

    y_out = nc.dram_tensor("y", [T, D], F32, kind="ExternalOutput").ap()
    ik = "ExternalOutput" if debug else "Internal"
    x1s = nc.dram_tensor("x1s", [T, D], F32, kind=ik).ap()
    xqs = nc.dram_tensor("xqs", [T, D], BF16, kind=ik).ap()
    hqs = nc.dram_tensor("hqs", [NT1, HC, 128, T1], BF16, kind=ik).ap()
    f1s = nc.dram_tensor("f1s", [T, 1], F32, kind=ik).ap()
    vrow_d = nc.dram_tensor("vrow_d", [1, D], F32, kind="Internal").ap()
    if debug:
        dbg_u = nc.dram_tensor("dbg_u", [H, T], F32, kind="ExternalOutput").ap()
        dbg_mx = nc.dram_tensor("dbg_mx", [T, 2], F32, kind="ExternalOutput").ap()
        dbg_z = nc.dram_tensor("dbg_z", [H, T], F32, kind="ExternalOutput").ap()
        dbg_i = nc.dram_tensor("dbg_i", [H, T], I32, kind="ExternalOutput").ap()

    ident_d = nc.inline_tensor(np.eye(128, dtype=ml_dtypes.bfloat16), "c_ident")
    identf_d = nc.inline_tensor(np.eye(128, dtype=np.float32), "c_identf")
    esel_np = np.zeros((128, 128), dtype=np.float32)
    esel_np[0, :] = 1.0
    esel_d = nc.inline_tensor(esel_np, "c_esel")
    ones_d = nc.inline_tensor(np.ones((128, 1), dtype=np.float32), "c_ones")
    c128_d = nc.inline_tensor(np.full((128, 1), 128.0, dtype=np.float32),
                              "c_c128")

    stack = ExitStack()
    E = stack.enter_context

    def sb(name, shape, dtype):
        return E(nc.sbuf_tensor(name, list(shape), dtype))

    ident = sb("ident", (128, 128), BF16)
    identf = sb("identf", (128, 128), F32)
    esel = sb("esel", (128, 128), F32)
    onescol = sb("onescol", (128, 1), F32)
    c128 = sb("c128", (128, 1), F32)
    gammaBC = sb("gammaBC", (128, D), F32)
    betaBC = sb("betaBC", (128, D), F32)
    bfcBC = sb("bfcBC", (128, D), F32)
    rowbuf = sb("rowbuf", (128, 512), F32)
    cwcol = sb("cwcol", (128, 3 * HC), F32)
    cbcol = sb("cbcol", (128, HC), F32)
    f2cols = sb("f2cols", (128, T // 128), F32)
    wsmat = sb("wsmat", (128, 16), F32)
    wqcols = sb("wqcols", (128, 8), F32)
    cols = sb("cols", (128, 48), F32)
    colsb = sb("colsb", (128, 48), F32)
    wbuf = sb("wbuf", (128, DC * D), BF16)

    ps_s = [E(nc.psum_tensor(f"ps_s{i}", [128, 512], F32)) for i in range(4)]
    ps_a = E(nc.psum_tensor("ps_a", [128, 512], F32))
    ps_b = E(nc.psum_tensor("ps_b", [128, 512], F32))
    ps_c = E(nc.psum_tensor("ps_c", [128, 512], F32))
    ps_d = E(nc.psum_tensor("ps_d", [128, 512], F32))

    dS = Sem(E(nc.semaphore("dS")))
    dSb = Sem(E(nc.semaphore("dSb")))
    dOb = Sem(E(nc.semaphore("dOb")))
    dX = Sem(E(nc.semaphore("dX")))
    dXb = Sem(E(nc.semaphore("dXb")))
    dG = Sem(E(nc.semaphore("dG")))
    dO = Sem(E(nc.semaphore("dO")))
    mm = Sem(E(nc.semaphore("mm")))
    vv = Sem(E(nc.semaphore("vv")))
    ssm = Sem(E(nc.semaphore("ssm")))
    gg = Sem(E(nc.semaphore("gg")))

    p = PB(nc, "P0")
    for dst, srcd in ((ident, ident_d), (identf, identf_d), (esel, esel_d),
                      (onescol, ones_d), (c128, c128_d)):
        p.op("sync", lambda s, dst=dst, srcd=srcd: s.dma_start(
            out=dst[:], in_=srcd.ap()), inc=dS, k=16)
    p.op("sync", lambda s: s.dma_start(
        out=gammaBC[:], in_=_ap(gamma_d.tensor, 0, [[0, 128], [1, D]])),
        inc=dS, k=16)
    p.op("sync", lambda s: s.dma_start(
        out=betaBC[:], in_=_ap(beta_d.tensor, 0, [[0, 128], [1, D]])),
        inc=dS, k=16)
    p.op("sync", lambda s: s.dma_start(
        out=bfcBC[:], in_=_ap(bfc_d.tensor, 0, [[0, 128], [1, D]])),
        inc=dS, k=16)
    for c in range(HC):
        p.op("sync", lambda s, c=c: s.dma_start(
            out=cwcol[:, 3 * c:3 * c + 3], in_=cw_d[c * 128:(c + 1) * 128, :]),
            inc=dS, k=16)
    for c in range(HC):
        p.op("sync", lambda s, c=c: s.dma_start(
            out=cbcol[:, c:c + 1], in_=cb_d[c * 128:(c + 1) * 128, :]),
            inc=dS, k=16)
    p.wait("sync", dS, dS.n)
    p.op("vector", lambda v: v.memset(rowbuf[:], 0.0))
    p.op("vector", lambda v: v.memset(wsmat[:], 0.0))
    p.run()

    with nc.sbuf_tensor("wstage", [128, H], F32) as wstage, \
         nc.sbuf_tensor("wstage2", [128, H], F32) as wstage2, \
         nc.sbuf_tensor("wabs", [128, H], F32) as wabs:
        p = PB(nc, "W0")
        p.wait("sync", dS, dS.n)
        p.wait("sync", dSb, dSb.n)
        chunks = [(w1T[dc * 128:(dc + 1) * 128, :], H) for dc in range(DC)] + \
                 [(w2T[hc * 128:(hc + 1) * 128, :], D) for hc in range(HC)]
        NCH = len(chunks)
        for i, (src, width) in enumerate(chunks):
            stage = wstage if i % 2 == 0 else wstage2
            sem = dS if i % 2 == 0 else dSb
            if i >= 2:
                p.wait("sync", ssm, ssm.n - 1)
            p.op("sync", lambda s, stage=stage, src=src, width=width:
                 s.dma_start(out=stage[:, 0:width], in_=src), inc=sem, k=16)
            p.wait("scalar", sem, sem.n)
            if i >= 1:
                p.wait("scalar", vv, vv.n)
            p.op("scalar", lambda sc, stage=stage, width=width: sc.activation(
                out=wabs[:, 0:width], in_=stage[:, 0:width], func=AF.Abs), inc=ssm)
            p.wait("vector", ssm, ssm.n)
            p.op("vector", lambda v, i=i, width=width: v.reduce_sum(
                out=colsb[:, i:i + 1], in_=wabs[:, 0:width], axis=AX.X), inc=vv)
        p.op("vector", lambda v: v.reduce_sum(
            out=cols[:, 0:1], in_=colsb[:, 0:DC], axis=AX.X))
        p.op("vector", lambda v: v.reduce_sum(
            out=cols[:, 1:2], in_=colsb[:, DC:DC + HC], axis=AX.X), inc=vv)
        p.wait("tensor", vv, vv.n)
        p.op("tensor", lambda t: t.matmul(ps_c[0:1, 0:1], onescol[:], cols[:, 0:1],
                                          start=True, stop=True))
        p.op("tensor", lambda t: t.matmul(ps_c[0:1, 1:2], onescol[:], cols[:, 1:2],
                                          start=True, stop=True), inc=mm)
        p.wait("vector", mm, mm.n)
        p.op("vector", lambda v: v.tensor_copy(out=wsmat[0:1, 0:2], in_=ps_c[0:1, 0:2]))
        p.op("vector", lambda v: v.tensor_scalar(
            out=wsmat[0:1, 2:3], in0=wsmat[0:1, 0:1], scalar1=1.0 / (D * H),
            scalar2=1e-5, op0=ALU.mult, op1=ALU.max))
        p.op("vector", lambda v: v.tensor_scalar(
            out=wsmat[0:1, 3:4], in0=wsmat[0:1, 1:2], scalar1=1.0 / (D * H),
            scalar2=1e-5, op0=ALU.mult, op1=ALU.max))
        p.op("vector", lambda v: v.reciprocal(out=wsmat[0:1, 4:5], in_=wsmat[0:1, 2:3]))
        p.op("vector", lambda v: v.reciprocal(out=wsmat[0:1, 5:6], in_=wsmat[0:1, 3:4]))
        p.op("vector", lambda v: v.reciprocal(out=wsmat[0:1, 6:7], in_=wsmat[0:1, 4:5]))
        p.op("vector", lambda v: v.reciprocal(out=wsmat[0:1, 7:8], in_=wsmat[0:1, 5:6]))
        p.op("vector", lambda v: v.tensor_scalar(
            out=wsmat[0:1, 8:9], in0=wsmat[0:1, 6:7], scalar1=-1.0, scalar2=None,
            op0=ALU.mult))
        p.op("vector", lambda v: v.tensor_scalar(
            out=wsmat[0:1, 9:10], in0=wsmat[0:1, 7:8], scalar1=-1.0, scalar2=None,
            op0=ALU.mult), inc=vv)
        p.wait("tensor", vv, vv.n)
        for j, cell in enumerate((4, 5, 8, 9)):
            p.op("tensor", lambda t, j=j, cell=cell: t.matmul(
                ps_d[:, j:j + 1], esel[:], wsmat[:, cell:cell + 1],
                start=True, stop=True), inc=(mm if j == 3 else None))
        p.wait("vector", mm, mm.n)
        p.op("vector", lambda v: v.tensor_copy(out=wqcols[:, 0:4], in_=ps_d[:, 0:4]),
             inc=vv)
        p.run()

    def emit_epilogue(p, row0, psA, psB, mm_wait, eb, kind, par, dep_store=None,
                      sto=None, ldsem=None):
        xres, htile, atile, ln1t, sqsc, x1t, iqt, iqtb, xqt = eb
        if sto is None:
            sto = dO
        c0 = 24 * par
        xr = xres[:, par * D:(par + 1) * D]
        x1 = x1t[:, par * D:(par + 1) * D]
        xq = xqt[:, par * D:(par + 1) * D]

        def C(j):
            return cols[:, c0 + j:c0 + j + 1]

        p.wait("vector", mm, mm_wait)
        p.wait("vector", ldsem, ldsem.n)
        if kind == "ffn":
            f2c = f2cols[:, row0 // 128: row0 // 128 + 1]
            p.op("vector", lambda v: v.tensor_scalar(
                out=htile[:, 0:DH], in0=psA[:, 0:DH], scalar1=f2c, scalar2=None,
                op0=ALU.mult))
            if NDH > 1:
                p.op("vector", lambda v: v.tensor_scalar(
                    out=htile[:, DH:D], in0=psB[:, 0:DH], scalar1=f2c, scalar2=None,
                    op0=ALU.mult))
            p.op("vector", lambda v: v.tensor_tensor(
                out=atile[:], in0=htile[:], in1=xr, op=ALU.add), inc=vv)
            ev_h1 = vv.n
        else:
            p.op("vector", lambda v: v.tensor_tensor(
                out=htile[:], in0=atile[:], in1=bfcBC[:], op=ALU.add))
            p.op("vector", lambda v: v.tensor_tensor(
                out=atile[:], in0=htile[:], in1=xr, op=ALU.add), inc=vv)
            ev_h1 = vv.n
        p.op("vector", lambda v: v.reduce_sum(out=C(0), in_=atile[:], axis=AX.X))
        p.wait("scalar", vv, ev_h1)
        p.op("scalar", lambda sc: sc.activation(
            out=sqsc[:], in_=atile[:], func=AF.Square, accum_out=C(1)), inc=ssm)
        ss_sq1 = ssm.n
        p.wait("vector", ssm, ss_sq1)
        p.op("vector", lambda v: v.tensor_scalar(
            out=C(2), in0=C(0), scalar1=1.0 / D, scalar2=None, op0=ALU.mult))
        p.op("vector", lambda v: v.tensor_scalar(
            out=C(3), in0=C(1), scalar1=1.0 / D, scalar2=EPS, op0=ALU.mult,
            op1=ALU.add))
        p.op("vector", lambda v: v.tensor_scalar(
            out=C(4), in0=C(2), scalar1=C(2), scalar2=-1.0, op0=ALU.mult,
            op1=ALU.mult))
        p.op("vector", lambda v: v.tensor_tensor(
            out=C(5), in0=C(3), in1=C(4), op=ALU.add), inc=vv)
        ev_var1 = vv.n
        p.wait("scalar", vv, ev_var1)
        p.op("scalar", lambda sc: sc.activation(out=C(6), in_=C(5), func=AF.Ln))
        p.op("scalar", lambda sc: sc.activation(out=C(7), in_=C(6), func=AF.Exp,
                                                scale=-0.5), inc=ssm)
        ss_sd1 = ssm.n
        p.wait("vector", ssm, ss_sd1)
        if kind == "ffn":
            if dep_store is not None:
                p.wait("vector", sto, dep_store)
            p.op("vector", lambda v: v.tensor_scalar(
                out=x1, in0=atile[:], scalar1=C(2), scalar2=C(7),
                op0=ALU.subtract, op1=ALU.mult), inc=vv)
            ev_out = vv.n
            p.wait("scalar", vv, ev_out)
            p.op("scalar", lambda sc: sc.dma_start(
                out=y_out[row0:row0 + 128, :], in_=x1), inc=sto, k=16)
            return ev_h1
        p.op("vector", lambda v: v.tensor_scalar(
            out=ln1t[:], in0=atile[:], scalar1=C(2), scalar2=C(7),
            op0=ALU.subtract, op1=ALU.mult))
        p.op("vector", lambda v: v.tensor_tensor(
            out=htile[:], in0=ln1t[:], in1=gammaBC[:], op=ALU.mult))
        p.op("vector", lambda v: v.tensor_tensor(
            out=ln1t[:], in0=htile[:], in1=betaBC[:], op=ALU.add))
        p.op("vector", lambda v: v.tensor_tensor(
            out=htile[:], in0=ln1t[:], in1=xr, op=ALU.add), inc=vv)
        ev_h2 = vv.n
        p.op("vector", lambda v: v.reduce_sum(out=C(8), in_=htile[:], axis=AX.X))
        p.wait("scalar", vv, ev_h2)
        p.op("scalar", lambda sc: sc.activation(
            out=sqsc[:], in_=htile[:], func=AF.Square, accum_out=C(9)), inc=ssm)
        ss_sq2 = ssm.n
        p.wait("vector", ssm, ss_sq2)
        p.op("vector", lambda v: v.tensor_scalar(
            out=C(10), in0=C(8), scalar1=1.0 / D, scalar2=None, op0=ALU.mult))
        p.op("vector", lambda v: v.tensor_scalar(
            out=C(11), in0=C(9), scalar1=1.0 / D, scalar2=EPS, op0=ALU.mult,
            op1=ALU.add))
        p.op("vector", lambda v: v.tensor_scalar(
            out=C(12), in0=C(10), scalar1=C(10), scalar2=-1.0, op0=ALU.mult,
            op1=ALU.mult))
        p.op("vector", lambda v: v.tensor_tensor(
            out=C(13), in0=C(11), in1=C(12), op=ALU.add), inc=vv)
        ev_var2 = vv.n
        p.wait("scalar", vv, ev_var2)
        p.op("scalar", lambda sc: sc.activation(out=C(14), in_=C(13), func=AF.Ln))
        p.op("scalar", lambda sc: sc.activation(out=C(15), in_=C(14), func=AF.Exp,
                                                scale=-0.5), inc=ssm)
        ss_sd2 = ssm.n
        p.wait("vector", ssm, ss_sd2)
        if dep_store is not None:
            p.wait("vector", sto, dep_store)
        p.op("vector", lambda v: v.tensor_scalar(
            out=x1, in0=htile[:], scalar1=C(10), scalar2=C(15),
            op0=ALU.subtract, op1=ALU.mult), inc=vv)
        ev_x1 = vv.n
        p.wait("scalar", vv, ev_x1)
        p.op("scalar", lambda sc: sc.dma_start(
            out=x1s[row0:row0 + 128, :], in_=x1), inc=sto, k=16)
        p.op("vector", lambda v: v.reduce_max(
            out=C(16), in_=x1, axis=AX.X, apply_absolute_value=True))
        p.op("vector", lambda v: v.tensor_scalar(
            out=C(17), in0=C(16), scalar1=1e-5, scalar2=None, op0=ALU.max))
        p.op("vector", lambda v: v.reciprocal(out=C(18), in_=C(17)))
        p.op("vector", lambda v: v.tensor_scalar(
            out=C(19), in0=C(18), scalar1=127.0, scalar2=None, op0=ALU.mult))
        p.op("vector", lambda v: v.reciprocal(out=C(20), in_=C(19)), inc=vv)
        ev_scale = vv.n
        p.wait("scalar", vv, ev_scale)
        p.op("scalar", lambda sc: sc.dma_start(
            out=f1s[row0:row0 + 128, :], in_=C(20)), inc=sto, k=16)
        p.op("vector", lambda v: v.tensor_scalar(
            out=sqsc[:], in0=x1, scalar1=C(19), scalar2=128.0, op0=ALU.mult,
            op1=ALU.add))
        p.op("vector", lambda v: v.tensor_copy(out=iqt[:], in_=sqsc[:]))
        p.op("vector", lambda v: v.tensor_tensor(
            out=ln1t[:], in0=sqsc[:], in1=iqt[:], op=ALU.subtract))
        p.op("vector", lambda v: v.scalar_tensor_tensor(
            out=htile[:], in0=ln1t[:], scalar=0.5, in1=iqt[:], op0=ALU.is_ge,
            op1=ALU.add))
        p.op("vector", lambda v: v.tensor_scalar(
            out=xq, in0=htile[:], scalar1=-128.0, scalar2=None, op0=ALU.add),
            inc=vv)
        ev_xq = vv.n
        p.wait("scalar", vv, ev_xq)
        p.op("scalar", lambda sc: sc.dma_start(
            out=xqs[row0:row0 + 128, :], in_=xq), inc=sto, k=16)
        return ev_h2

    attn_stack = ExitStack()
    qf = attn_stack.enter_context(nc.sbuf_tensor("qf", [128, DC * S], BF16))
    kf = attn_stack.enter_context(nc.sbuf_tensor("kf", [128, DC * S], BF16))
    wrowbc = attn_stack.enter_context(nc.sbuf_tensor("wrowbc", [128, D], F32))
    vrow = attn_stack.enter_context(nc.sbuf_tensor("vrow", [1, D], F32))
    wf32 = attn_stack.enter_context(nc.sbuf_tensor("wf32", [128, 2 * D], F32))

    def wload(p, src):
        for dc in range(DC):
            p.op("gpsimd", lambda g, dc=dc, src=src: g.dma_start(
                out=wbuf[:, dc * D:(dc + 1) * D],
                in_=src[dc * 128:(dc + 1) * 128, :]), inc=dG, k=16)

    def mm_evac(p, items, emit_group, emit_evac, banks):
        nb = len(banks)
        evbase = vv.n
        mmbase = mm.n
        for i, it in enumerate(items):
            if i >= nb:
                p.wait("tensor", vv, evbase + (i - nb + 1))
            emit_group(p, it, banks[i % nb])
            assert mm.n == mmbase + i + 1
            p.wait("vector", mm, mm.n)
            emit_evac(p, it, banks[i % nb])
            assert vv.n == evbase + i + 1

    xf = attn_stack.enter_context(nc.sbuf_tensor("xfb", [128, DC * S], BF16))
    dGq = {}

    def emit_A0(p, s_i):
        for dc in range(DC):
            p.op("gpsimd", lambda g, dc=dc, s_i=s_i: g.dma_start(
                out=xf[:, dc * S:(dc + 1) * S],
                in_=x_fm[s_i, dc * 128:(dc + 1) * 128, :]), inc=dG, k=16)
        wload(p, wqT)
        dGq[s_i] = dG.n

    for s_i in range(NSEQ):
        if True:
            p = PB(nc, f"APRO_{s_i}")
            if s_i not in dGq:
                emit_A0(p, s_i)
            p.wait("gpsimd", dG, dGq[s_i])
            p.wait("tensor", dG, dGq[s_i])

            q_items = [(cb, ec) for cb in range(S // KBW) for ec in range(DC)]

            def q_group(p, it, bank):
                cb, ec = it
                for dc in range(DC):
                    p.op("tensor", lambda t, cb=cb, ec=ec, dc=dc, bank=bank:
                         t.matmul(bank[:, 0:KBW],
                                  wbuf[:, dc * D + ec * 128: dc * D + ec * 128 + 128],
                                  xf[:, dc * S + cb * KBW: dc * S + (cb + 1) * KBW],
                                  start=(dc == 0), stop=(dc == DC - 1)),
                         inc=(mm if dc == DC - 1 else None))

            def q_evac(p, it, bank):
                cb, ec = it
                p.op("vector", lambda v, cb=cb, ec=ec, bank=bank: v.tensor_scalar(
                    out=qf[:, ec * S + cb * KBW: ec * S + (cb + 1) * KBW],
                    in0=bank[:, 0:KBW], scalar1=cfg.QSCALE, scalar2=None,
                    op0=ALU.mult), inc=vv)

            mm_evac(p, q_items, q_group, q_evac, [ps_a, ps_b])

            p.wait("gpsimd", mm, mm.n)
            wload(p, wkT)
            p.wait("tensor", dG, dG.n)
            p.wait("tensor", vv, vv.n)

            def k_evac(p, it, bank):
                cb, ec = it
                p.op("vector", lambda v, cb=cb, ec=ec, bank=bank: v.tensor_copy(
                    out=kf[:, ec * S + cb * KBW: ec * S + (cb + 1) * KBW],
                    in_=bank[:, 0:KBW]), inc=vv)

            mm_evac(p, q_items, q_group, k_evac, [ps_a, ps_b])

            for dc in range(DC):
                p.op("vector", lambda v, dc=dc: v.reduce_sum(
                    out=colsb[:, 32 + dc:33 + dc], in_=xf[:, dc * S:(dc + 1) * S],
                    axis=AX.X), inc=(vv if dc == DC - 1 else None))
            p.wait("tensor", vv, vv.n)
            p.wait("sync", dS, dS.n)
            p.wait("sync", dSb, dSb.n)
            mvb = mm.n
            for dc in range(DC):
                sem = dS if dc % 2 == 0 else dSb
                if dc >= 2:
                    p.wait("sync", mm, mvb + NDH * (dc - 1))
                    p.wait("sync", sem, sem.n)
                p.op("sync", lambda s, dc=dc: s.dma_start(
                    out=wf32[:, (dc % 2) * D:(dc % 2 + 1) * D],
                    in_=wvT[dc * 128:(dc + 1) * 128, :]), inc=sem, k=16)
                p.wait("tensor", sem, sem.n)
                for h in range(NDH):
                    bank = ps_c if h == 0 else ps_d
                    p.op("tensor", lambda t, h=h, dc=dc, bank=bank: t.matmul(
                        bank[0:1, 0:DH], colsb[:, 32 + dc:33 + dc],
                        wf32[:, (dc % 2) * D + h * DH: (dc % 2) * D + (h + 1) * DH],
                        start=(dc == 0), stop=(dc == DC - 1)), inc=mm)
            p.wait("vector", mm, mm.n)
            for h in range(NDH):
                bank = ps_c if h == 0 else ps_d
                p.op("vector", lambda v, h=h, bank=bank: v.tensor_copy(
                    out=vrow[0:1, h * DH:(h + 1) * DH], in_=bank[0:1, 0:DH]),
                    inc=(vv if h == NDH - 1 else None))
            p.wait("sync", vv, vv.n)
            p.op("sync", lambda s: s.dma_start(out=vrow_d, in_=vrow[0:1, :]),
                 inc=dS, k=16)
            p.wait("sync", dS, dS.n)
            for dc in range(DC):
                p.op("sync", lambda s, dc=dc: s.dma_start(
                    out=colsb[:, 40 + dc:41 + dc],
                    in_=_ap(vrow_d.tensor, dc * 128, [[1, 128], [1, 1]])),
                    inc=dS, k=16)

        p.wait("sync", dS, dS.n)
        p.wait("sync", dSb, dSb.n)
        p.wait("tensor", dS, dS.n)
        p.wait("tensor", dSb, dSb.n)
        mvb = mm.n
        for dc in range(DC):
            sem = dS if dc % 2 == 0 else dSb
            if dc >= 2:
                p.wait("sync", mm, mvb + NDH * (dc - 1))
                p.wait("sync", sem, sem.n)
            p.op("sync", lambda s, dc=dc: s.dma_start(
                out=wf32[:, (dc % 2) * D:(dc % 2 + 1) * D],
                in_=wfcT[dc * 128:(dc + 1) * 128, :]), inc=sem, k=16)
            p.wait("tensor", sem, sem.n)
            for h in range(NDH):
                bank = ps_c if h == 0 else ps_d
                p.op("tensor", lambda t, h=h, dc=dc, bank=bank: t.matmul(
                    bank[0:1, 0:DH], colsb[:, 40 + dc:41 + dc],
                    wf32[:, (dc % 2) * D + h * DH: (dc % 2) * D + (h + 1) * DH],
                    start=(dc == 0), stop=(dc == DC - 1)), inc=mm)
        for h in range(NDH):
            bank = ps_c if h == 0 else ps_d
            p.wait("vector", mm, mm.n - (NDH - 1 - h))
            p.op("vector", lambda v, h=h, bank=bank: v.tensor_copy(
                out=rowbuf[0:1, 0:DH], in_=bank[0:1, 0:DH]), inc=vv)
            ev_row = vv.n
            p.wait("tensor", vv, ev_row)
            p.op("tensor", lambda t, h=h: t.matmul(
                ps_s[0][:, 0:DH], esel[:], rowbuf[:, 0:DH],
                start=True, stop=True), inc=mm)
            p.wait("vector", mm, mm.n)
            p.op("vector", lambda v, h=h: v.tensor_copy(
                out=wrowbc[:, h * DH:(h + 1) * DH], in_=ps_s[0][:, 0:DH]),
                inc=vv)
        p.run()

        a3s = ExitStack()
        Ea = a3s.enter_context
        disc = Ea(nc.sbuf_tensor(f"disc{s_i}", [128, KBW], F32))
        xres = Ea(nc.sbuf_tensor(f"xres{s_i}", [128, 2 * D], F32))
        htile = Ea(nc.sbuf_tensor(f"htile{s_i}", [128, D], F32))
        atile = Ea(nc.sbuf_tensor(f"atile{s_i}", [128, D], F32))
        ln1t = Ea(nc.sbuf_tensor(f"ln1t{s_i}", [128, D], F32))
        sqsc = Ea(nc.sbuf_tensor(f"sqsc{s_i}", [128, D], F32))
        x1t = Ea(nc.sbuf_tensor(f"x1t{s_i}", [128, 2 * D], F32))
        iqt = Ea(nc.sbuf_tensor(f"iqt{s_i}", [128, D], I32))
        iqtb = Ea(nc.sbuf_tensor(f"iqtb{s_i}", [128, D], I32))
        xqt = Ea(nc.sbuf_tensor(f"xqt{s_i}", [128, 2 * D], BF16))
        eb = (xres, htile, atile, ln1t, sqsc, x1t, iqt, iqtb, xqt)
        if True:
            if True:
                p = PB(nc, f"A3_{s_i}")
                if s_i + 1 < NSEQ:
                    emit_A0(p, s_i + 1)
                dO_bases = {0: dO.n, 1: dOb.n}
                exp_ev = {}
                mm_at = {}
                epi_q = {}
                for qt in range(NGRP * NQT):
                    row0 = s_i * S + qt * 128
                    qcol = qt * 128
                    ldsem = dX if qt % 2 == 0 else dXb
                    if qt >= 2:
                        p.wait("sync", vv, ev_h2_list[qt - 2])
                    p.op("sync", lambda s, row0=row0, qt=qt: s.dma_start(
                        out=xres[:, (qt % 2) * D:(qt % 2 + 1) * D],
                        in_=x_tm[row0:row0 + 128, :]), inc=ldsem, k=16)
                    if qt > 0:
                        p.wait("tensor", ssm, exp_ev[qt - 1])
                    for kb in range(KB):
                        for dc in range(DC):
                            p.op("tensor", lambda t, qcol=qcol, kb=kb, dc=dc:
                                 t.matmul(
                                     ps_s[kb][:, 0:KBW],
                                     qf[:, dc * S + qcol: dc * S + qcol + 128],
                                     kf[:, dc * S + kb * KBW: dc * S + (kb + 1) * KBW],
                                     start=(dc == 0), stop=(dc == DC - 1)),
                                 inc=(mm if kb == KB - 1 and dc == DC - 1 else None))
                    mm_at[qt] = mm.n
                    p.wait("vector", mm, mm.n)
                    for kb in range(KB):
                        p.op("vector", lambda v, kb=kb: v.reduce_max(
                            out=colsb[:, kb:kb + 1], in_=ps_s[kb][:, 0:KBW],
                            axis=AX.X, negate=True))
                    p.op("vector", lambda v: v.tensor_reduce(
                        out=colsb[:, 8:9], in_=colsb[:, 0:KB], axis=AX.X,
                        op=ALU.min), inc=vv)
                    ev_negm = vv.n
                    p.wait("scalar", vv, ev_negm)
                    for kb in range(KB):
                        p.op("scalar", lambda sc, kb=kb: sc.activation(
                            out=disc[:], in_=ps_s[kb][:, 0:KBW], func=AF.Exp,
                            bias=colsb[:, 8:9], scale=1.0,
                            accum_out=colsb[:, 16 + kb:17 + kb]),
                            inc=(ssm if kb == KB - 1 else None))
                    exp_ev[qt] = ssm.n
                    if qt >= 1:
                        row0p, parp, stop, depp, ldp = epi_q[qt - 1]
                        ev = emit_epilogue(p, row0p, None, None, mm_at[qt - 1],
                                           eb, "attn", parp, dep_store=depp,
                                           sto=stop, ldsem=ldp)
                        ev_h2_list.append(ev)
                    p.wait("vector", ssm, exp_ev[qt])
                    p.op("vector", lambda v: v.reduce_sum(
                        out=colsb[:, 24:25], in_=colsb[:, 16:16 + KB], axis=AX.X))
                    p.op("vector", lambda v: v.reciprocal(
                        out=colsb[:, 25:26], in_=colsb[:, 24:25]))
                    p.op("vector", lambda v: v.tensor_scalar(
                        out=colsb[:, 26:27], in0=colsb[:, 24:25],
                        scalar1=colsb[:, 25:26], scalar2=None, op0=ALU.mult))
                    p.op("vector", lambda v: v.tensor_scalar(
                        out=atile[:], in0=wrowbc[:], scalar1=colsb[:, 26:27],
                        scalar2=None, op0=ALU.mult), inc=vv)
                    par = qt % 2
                    sto = dO if par == 0 else dOb
                    dep = dO_bases[par] + (48 * (qt // 2) if qt >= 2 else 0)
                    if qt == 0:
                        ev_h2_list = []
                    epi_q[qt] = (row0, par, sto, dep, ldsem)
                qt_last = NGRP * NQT - 1
                row0p, parp, stop, depp, ldp = epi_q[qt_last]
                ev = emit_epilogue(p, row0p, None, None, mm_at[qt_last], eb,
                                   "attn", parp, dep_store=depp, sto=stop,
                                   ldsem=ldp)
                ev_h2_list.append(ev)
                p.run()
        p = PB(nc, f"A3drain_{s_i}")
        for eng in ("sync", "scalar", "vector", "tensor", "gpsimd"):
            p.wait(eng, dO, dO.n)
            p.wait(eng, dOb, dOb.n)
            p.wait(eng, dX, dX.n)
            p.wait(eng, dXb, dXb.n)
        p.run()
        a3s.close()

    attn_stack.close()

    tw = stack.enter_context(nc.sbuf_tensor("tw", [128, DC * H], BF16))

    def quantize_weights(srcT, nch, width, dst_off):
        with nc.sbuf_tensor(f"qstage_{dst_off}", [128, width], F32) as qstage, \
             nc.sbuf_tensor(f"qstage2_{dst_off}", [128, width], F32) as qstage2, \
             nc.sbuf_tensor(f"qz_{dst_off}", [128, width], F32) as qz, \
             nc.sbuf_tensor(f"qa_{dst_off}", [128, width], F32) as qa:
            p = PB(nc, "W1")
            p.wait("sync", dS, dS.n)
            p.wait("sync", dSb, dSb.n)
            if debug:
                for eng in ("sync", "vector"):
                    p.wait(eng, dOb, dOb.n)
                    p.wait(eng, dO, dO.n)
            scol = wqcols[:, 0:1] if dst_off == 0 else wqcols[:, 1:2]
            for c in range(nch):
                stage = qstage if c % 2 == 0 else qstage2
                sem = dS if c % 2 == 0 else dSb
                if c >= 2:
                    p.wait("sync", vv, vv.n - 3)
                p.op("sync", lambda s, c=c, stage=stage: s.dma_start(
                    out=stage[:, 0:width], in_=srcT[c * 128:(c + 1) * 128, :]),
                    inc=sem, k=16)
                p.wait("vector", sem, sem.n)
                p.op("vector", lambda v, stage=stage: v.tensor_scalar(
                    out=qz[:, 0:width], in0=stage[:, 0:width], scalar1=scol,
                    scalar2=None, op0=ALU.mult), inc=vv)
                p.op("vector", lambda v: v.tensor_scalar(
                    out=qa[:, 0:width], in0=qz[:, 0:width], scalar1=0.5,
                    scalar2=None, op0=ALU.is_ge), inc=vv)
                p.op("vector", lambda v, c=c: v.scalar_tensor_tensor(
                    out=tw[:, c * width:(c + 1) * width], in0=qz[:, 0:width],
                    scalar=-0.5, in1=qa[:, 0:width], op0=ALU.is_le,
                    op1=ALU.subtract), inc=vv)
            p.run()

    quantize_weights(w1T, DC, H, 0)

    T1W2 = T1 + 2
    NSB1 = T1 // 128
    NQG = min(8, HC)
    WW = T1 + 32
    with nc.sbuf_tensor("xqF", [128, 2 * DC * WW], BF16) as xqF, \
         nc.sbuf_tensor("ubuf", [128, HC * T1W2], F32) as ubuf, \
         nc.sbuf_tensor("vt1", [128, 4 * T1W2], F32) as vt1, \
         nc.sbuf_tensor("vt2", [128, 2 * T1W2], F32) as vt2, \
         nc.sbuf_tensor("maxM", [128, T1], F32) as maxM, \
         nc.sbuf_tensor("rbuf", [128, NQG * T1], BF16) as rbuf, \
         nc.sbuf_tensor("f1bc", [128, 4 * T1W2], F32) as f1bc, \
         nc.sbuf_tensor("s2bc8", [128, NQG * T1], F32) as s2bc8, \
         nc.sbuf_tensor("zscr", [128, NQG * T1], F32) as zscr, \
         nc.sbuf_tensor("iqq", [128, NQG * T1], I32) as iqq, \
         nc.sbuf_tensor("dsb", [128, NQG * T1], BF16) as dsb, \
         nc.sbuf_tensor("accb", [128, NQG * T1], BF16) as accb, \
         nc.sbuf_tensor("hqst", [128, HC * T1], BF16) as hqst:
        p = PB(nc, "F1")
        p.wait("sync", dO, dO.n)
        p.wait("sync", dOb, dOb.n)
        p.wait("sync", dX, dX.n)
        p.wait("sync", dXb, dXb.n)
        p.wait("sync", dS, dS.n)

        def tgeom(i):
            s_i = (i * T1) // S
            tb = i * T1
            lo = max(tb - 1, s_i * S)
            hi = min(tb + T1 + 1, (s_i + 1) * S)
            return tb, lo, hi, hi - lo, lo - (tb - 1), min(max(tb - 16, 0), T - WW)

        ldsems = (dX, dXb)
        stsems = (dO, dOb)
        mm_after_u = {}
        lad = {}
        ev_u = {}
        ev_conv = {}
        ss_vt1 = {}
        ss_silu = {}
        st_cnt = {}
        ld_cnt = {}

        def emit_loads(i):
            tb, lo, hi, W, off, w0 = tgeom(i)
            par = i % 2
            sem = ldsems[par]
            if i >= 2:
                p.wait("sync", mm, mm_after_u[i - 2])
                p.wait("sync", sem, sem.n)
            for dc in range(DC):
                p.op("sync", lambda s, dc=dc, w0=w0, par=par: s.dma_start(
                    out=xqF[:, (par * DC + dc) * WW:(par * DC + dc + 1) * WW],
                    in_=xqs[w0:w0 + WW, dc * 128:(dc + 1) * 128], transpose=True),
                    inc=sem, k=16)
            p.op("sync", lambda s, lo=lo, W=W, par=par: s.dma_start(
                out=f1bc[:, 2 * par * T1W2:2 * par * T1W2 + W],
                in_=_ap(f1s.tensor, lo, [[0, 128], [1, W]])), inc=sem, k=16)
            ld_cnt[i] = sem.n

        def emit_conv_tail(i, hc):
            cq = hc % 4
            cpar = hc % 2
            p.wait("vector", ssm, ss_vt1[(i, hc)])
            p.op("vector", lambda v, hc=hc, cq=cq, cpar=cpar:
                 v.scalar_tensor_tensor(
                out=vt2[:, cpar * T1W2:cpar * T1W2 + T1],
                in0=ubuf[:, hc * T1W2:hc * T1W2 + T1],
                scalar=cwcol[:, 3 * hc:3 * hc + 1],
                in1=vt1[:, cq * T1W2:cq * T1W2 + T1],
                op0=ALU.mult, op1=ALU.add))
            p.op("vector", lambda v, hc=hc, cq=cq, cpar=cpar:
                 v.scalar_tensor_tensor(
                out=vt1[:, cq * T1W2:cq * T1W2 + T1],
                in0=ubuf[:, hc * T1W2 + 2:hc * T1W2 + 2 + T1],
                scalar=cwcol[:, 3 * hc + 2:3 * hc + 3],
                in1=vt2[:, cpar * T1W2:cpar * T1W2 + T1],
                op0=ALU.mult, op1=ALU.add), inc=vv)
            ev_conv[(i, hc)] = vv.n
            p.wait("scalar", vv, ev_conv[(i, hc)])
            p.op("scalar", lambda sc, hc=hc, cq=cq: sc.activation(
                out=ubuf[:, hc * T1W2 + 1:hc * T1W2 + 1 + T1],
                in_=vt1[:, cq * T1W2:cq * T1W2 + T1], func=AF.Silu),
                inc=ssm)
            ss_silu[(i, hc)] = ssm.n

        qg_iqq = {}

        def emit_quant_A(i, g0):
            ng = min(NQG, HC - g0)
            FD = ng * T1
            s28 = s2bc8[:, 0:FD]
            p.op("vector", lambda v, g0=g0, ng=ng, FD=FD, s28=s28: v.tensor_tensor(
                out=zscr[:, 0:FD],
                in0=_ap(ubuf, g0 * T1W2 + 1,
                        [[HC * T1W2, 128], [T1W2, ng], [1, T1]]),
                in1=s28, op=ALU.mult), inc=vv)
            ev_z = vv.n
            p.wait("scalar", vv, ev_z)
            if g0 // NQG >= 1:
                p.wait("scalar", vv, lad[(i, g0 // NQG - 1)])
            elif i >= 1:
                p.wait("scalar", vv, lad[(i - 1, (HC - 1) // NQG)])
            p.op("scalar", lambda sc, FD=FD: sc.activation(
                out=iqq[:, 0:FD], in_=zscr[:, 0:FD], func=AF.Identity,
                bias=c128[:, 0:1], scale=1.0), inc=ssm)
            qg_iqq[(i, g0 // NQG)] = ssm.n

        def emit_quant_B(i, g0):
            ng = min(NQG, HC - g0)
            FD = ng * T1
            if g0 == 0:
                if i >= 1:
                    p.wait("vector", stsems[(i - 1) % 2], st_cnt[i - 1])
            p.wait("vector", ssm, qg_iqq[(i, g0 // NQG)])
            p.op("vector", lambda v, FD=FD: v.scalar_tensor_tensor(
                out=dsb[:, 0:FD], in0=zscr[:, 0:FD], scalar=128.0,
                in1=iqq[:, 0:FD], op0=ALU.add, op1=ALU.subtract))
            p.op("vector", lambda v, FD=FD: v.tensor_scalar(
                out=accb[:, 0:FD], in0=dsb[:, 0:FD], scalar1=-1.5,
                scalar2=-130.0, op0=ALU.is_ge, op1=ALU.add))
            for thr in (-0.5, 0.5, 1.5, 2.5):
                p.op("vector", lambda v, FD=FD, thr=thr: v.tensor_scalar(
                    out=rbuf[:, 0:FD], in0=dsb[:, 0:FD], scalar1=thr,
                    scalar2=None, op0=ALU.is_ge))
                p.op("vector", lambda v, FD=FD: v.tensor_tensor(
                    out=accb[:, 0:FD], in0=accb[:, 0:FD], in1=rbuf[:, 0:FD],
                    op=ALU.add))
            p.op("vector", lambda v, g0=g0, FD=FD: v.tensor_tensor(
                out=hqst[:, g0 * T1:g0 * T1 + FD], in0=accb[:, 0:FD],
                in1=iqq[:, 0:FD], op=ALU.add), inc=vv)
            lad[(i, g0 // NQG)] = vv.n

        def emit_store(i):
            par = i % 2
            sem = stsems[par]
            p.wait("sync", vv, lad[(i, (HC - 1) // NQG)])
            p.wait("sync", sem, sem.n)
            p.op("sync", lambda s, i=i: s.dma_start(
                out=_ap(hqs.tensor, i * HC * 128 * T1,
                        [[T1, 128], [128 * T1, HC], [1, T1]]),
                in_=_ap(hqst, 0, [[HC * T1, 128], [T1, HC], [1, T1]])),
                inc=sem, k=16)
            st_cnt[i] = sem.n

        emit_loads(0)
        if NT1 > 1:
            emit_loads(1)
        for i in range(NT1):
            tb, lo, hi, W, off, w0 = tgeom(i)
            par = i % 2
            fb_raw = 2 * par * T1W2
            fb_s = fb_raw + T1W2
            p.wait("vector", ldsems[par], ld_cnt[i])
            p.op("vector", lambda v, fb_s=fb_s: v.memset(
                f1bc[:, fb_s:fb_s + T1W2], 0.0))
            p.op("vector", lambda v, fb_raw=fb_raw, fb_s=fb_s, off=off, W=W:
                 v.tensor_scalar(
                     out=f1bc[:, fb_s + off:fb_s + off + W],
                     in0=f1bc[:, fb_raw:fb_raw + W],
                     scalar1=wqcols[:, 2:3], scalar2=None, op0=ALU.mult), inc=vv)
            ev_f1bc = vv.n
            p.wait("tensor", vv, ev_f1bc)
            p.wait("tensor", ldsems[par], ld_cnt[i])
            for hp in range(HC // 2):
                hc0 = 2 * hp
                if i >= 1 and hc0 % NQG == 0:
                    emit_quant_A(i - 1, hc0)
                if i >= 1 and hc0 % NQG == NQG // 2:
                    emit_quant_B(i - 1, hc0 - NQG // 2)
                for hc in (hc0, hc0 + 1):
                    bank = (ps_a, ps_b, ps_s[0], ps_s[1])[hc % 4]
                    if hc >= 4:
                        p.wait("tensor", vv, ev_u[(i, hc - 4)])
                    elif i >= 1:
                        p.wait("tensor", vv, ev_u[(i - 1, HC - 4 + hc)])
                    for dc in range(DC):
                        p.op("tensor", lambda t, hc=hc, dc=dc, bank=bank,
                             par=par, lo=lo, hi=hi, w0=w0, W=W: t.matmul(
                            bank[:, 0:W],
                            tw[:, dc * H + hc * 128: dc * H + hc * 128 + 128],
                            xqF[:, (par * DC + dc) * WW + (lo - w0):
                                (par * DC + dc) * WW + (hi - w0)],
                            start=(dc == 0), stop=(dc == DC - 1)),
                            inc=(mm if dc == DC - 1 else None))
                    if hc == HC - 1:
                        mm_after_u[i] = mm.n
                for hc in (hc0 - 3, hc0 - 2):
                    if hc >= 0:
                        emit_conv_tail(i, hc)
                p.wait("vector", mm, mm.n)
                for hc in (hc0, hc0 + 1):
                    bank = (ps_a, ps_b, ps_s[0], ps_s[1])[hc % 4]
                    p.op("vector", lambda v, hc=hc, bank=bank, off=off, W=W,
                         fb_s=fb_s: v.tensor_tensor(
                        out=ubuf[:, hc * T1W2 + off: hc * T1W2 + off + W],
                        in0=bank[:, 0:W],
                        in1=f1bc[:, fb_s + off:fb_s + off + W], op=ALU.mult),
                        inc=vv)
                    ev_u[(i, hc)] = vv.n
                    if off == 1:
                        p.op("vector", lambda v, hc=hc: v.memset(
                            ubuf[:, hc * T1W2:hc * T1W2 + 1], 0.0))
                    if off + W < T1W2:
                        p.op("vector", lambda v, hc=hc: v.memset(
                            ubuf[:, hc * T1W2 + off + W:hc * T1W2 + T1W2],
                            0.0))
                for hc in (hc0, hc0 + 1):
                    p.wait("scalar", vv, ev_u[(i, hc)])
                    p.op("scalar", lambda sc, hc=hc: sc.activation(
                        out=vt1[:, (hc % 4) * T1W2:(hc % 4) * T1W2 + T1],
                        in_=ubuf[:, hc * T1W2 + 1:hc * T1W2 + 1 + T1],
                        func=AF.Identity,
                        scale=cwcol[:, 3 * hc + 1:3 * hc + 2],
                        bias=cbcol[:, hc:hc + 1]), inc=ssm)
                    ss_vt1[(i, hc)] = ssm.n
            for tl in range(HC - 3, HC):
                emit_conv_tail(i, tl)
            if i >= 1:
                emit_store(i - 1)
            if i + 2 < NT1:
                emit_loads(i + 2)
            p.wait("vector", ssm, ss_silu[(i, HC - 1)])
            p.op("vector", lambda v: v.reduce_max(
                out=maxM[:, 0:T1],
                in_=_ap(ubuf, 1, [[HC * T1W2, 128], [1, T1], [T1W2, HC]]),
                axis=AX.X, apply_absolute_value=True), inc=vv)
            ev_mabs = vv.n
            p.wait("tensor", vv, ev_mabs)
            for j in range(NSB1):
                p.op("tensor", lambda t, j=j: t.matmul(
                    ps_c[:, j * 128:(j + 1) * 128],
                    maxM[:, j * 128:(j + 1) * 128], identf[:],
                    start=True, stop=True), inc=(mm if j == NSB1 - 1 else None))
            p.wait("vector", mm, mm.n)
            for j in range(NSB1):
                p.op("vector", lambda v, j=j: v.reduce_max(
                    out=cols[:, 0:1], in_=ps_c[:, j * 128:(j + 1) * 128],
                    axis=AX.X))
                p.op("vector", lambda v, j=j: v.tensor_scalar(
                    out=cols[:, 1:2], in0=cols[:, 0:1], scalar1=1e-5,
                    scalar2=None, op0=ALU.max))
                p.op("vector", lambda v, j=j: v.reciprocal(
                    out=cols[:, 2:3], in_=cols[:, 1:2]))
                p.op("vector", lambda v, j=j: v.tensor_scalar(
                    out=cols[:, 3 + j:4 + j], in0=cols[:, 2:3], scalar1=127.0,
                    scalar2=None, op0=ALU.mult))
                p.op("vector", lambda v, j=j: v.reciprocal(
                    out=cols[:, 8:9], in_=cols[:, 3 + j:4 + j]))
                p.op("vector", lambda v, i=i, j=j: v.tensor_scalar(
                    out=f2cols[:, i * NSB1 + j:i * NSB1 + j + 1],
                    in0=cols[:, 8:9], scalar1=wqcols[:, 3:4], scalar2=None,
                    op0=ALU.mult), inc=vv)
            ev_scol = vv.n
            p.wait("tensor", vv, ev_scol)
            for j in range(NSB1):
                p.op("tensor", lambda t, j=j: t.matmul(
                    ps_d[0:1, j * 128:(j + 1) * 128],
                    cols[:, 3 + j:4 + j], identf[:], start=True, stop=True),
                    inc=(mm if j == NSB1 - 1 else None))
            p.wait("vector", mm, mm.n)
            p.op("vector", lambda v: v.tensor_copy(
                out=rowbuf[0:1, 0:T1], in_=ps_d[0:1, 0:T1]), inc=vv)
            ev_row = vv.n
            p.wait("tensor", vv, ev_row)
            p.op("tensor", lambda t: t.matmul(
                ps_d[:, 0:T1], esel[:], rowbuf[:, 0:T1], start=True, stop=True),
                inc=mm)
            p.wait("vector", mm, mm.n)
            p.op("vector", lambda v: v.tensor_copy(
                out=s2bc8[:, 0:T1], in_=ps_d[:, 0:T1]), inc=vv)
            w = T1
            while w < NQG * T1:
                cw2 = min(w, NQG * T1 - w)
                p.op("vector", lambda v, w=w, cw2=cw2: v.tensor_copy(
                    out=s2bc8[:, w:w + cw2], in_=s2bc8[:, 0:cw2]))
                w += cw2
            p.op("vector", lambda v: v.engine_nop(), inc=vv)
        for g0 in range(0, HC, NQG):
            emit_quant_A(NT1 - 1, g0)
            emit_quant_B(NT1 - 1, g0)
        emit_store(NT1 - 1)
        p.run()

    quantize_weights(w2T, HC, D, 1)

    NSB1F = T1 // 128
    NHQS = 3
    with nc.sbuf_tensor("hq2", [128, NHQS * HC * T1], BF16) as hq2, \
         nc.sbuf_tensor("xres2", [128, 2 * D], F32) as xres2, \
         nc.sbuf_tensor("htile2", [128, D], F32) as htile2, \
         nc.sbuf_tensor("atile2", [128, D], F32) as atile2, \
         nc.sbuf_tensor("ln1t2", [128, D], F32) as ln1t2, \
         nc.sbuf_tensor("sqsc2", [128, D], F32) as sqsc2, \
         nc.sbuf_tensor("x1t2", [128, 2 * D], F32) as x1t2, \
         nc.sbuf_tensor("iqt2", [128, D], I32) as iqt2, \
         nc.sbuf_tensor("iqt2b", [128, D], I32) as iqt2b, \
         nc.sbuf_tensor("xqt2", [128, 2 * D], BF16) as xqt2:
        eb2 = (xres2, htile2, atile2, ln1t2, sqsc2, x1t2, iqt2, iqt2b, xqt2)
        p = PB(nc, "F2")
        p.wait("sync", dS, dS.n)
        p.wait("sync", dSb, dSb.n)
        p.wait("sync", dO, dO.n)
        p.wait("sync", dOb, dOb.n)
        p.wait("sync", dX, dX.n)
        p.wait("sync", dXb, dXb.n)
        hq_ld = {}
        mm_after_f2 = {}
        st_cnt2 = {}
        dO_bases2 = {0: dO.n, 1: dOb.n}
        ev_h1_list = []

        def load_hq(i):
            if i >= NHQS:
                p.wait("sync", mm, mm_after_f2[i - NHQS])
            sem = dS if i % 2 == 0 else dSb
            p.wait("sync", sem, sem.n)
            p.op("sync", lambda s, i=i: s.dma_start(
                out=_ap(hq2, (i % NHQS) * HC * T1,
                        [[NHQS * HC * T1, 128], [T1, HC], [1, T1]]),
                in_=_ap(hqs.tensor, i * HC * 128 * T1,
                        [[T1, 128], [128 * T1, HC], [1, T1]])), inc=sem, k=16)
            hq_ld[i] = (sem, sem.n)

        for i in range(min(NHQS, NT1)):
            load_hq(i)
        for i in range(NT1):
            slot = (i % NHQS) * HC * T1
            for sq in range(NSB1F):
                st = i * NSB1F + sq
                row0 = i * T1 + sq * 128
                ldsem = dX if st % 2 == 0 else dXb
                if st >= 2:
                    p.wait("sync", vv, ev_h1_list[st - 2])
                p.op("sync", lambda s, row0=row0, st=st: s.dma_start(
                    out=xres2[:, (st % 2) * D:(st % 2 + 1) * D],
                    in_=x1s[row0:row0 + 128, :]), inc=ldsem, k=16)
                if sq == 0:
                    sem, cnt = hq_ld[i]
                    p.wait("tensor", sem, cnt)
                if st >= 2:
                    p.wait("tensor", vv, ev_h1_list[st - 2])
                banks = (ps_c, ps_d) if st % 2 == 0 else (ps_s[0], ps_s[1])
                for hc in range(HC):
                    last = (hc == HC - 1)
                    p.op("tensor", lambda t, slot=slot, sq=sq, hc=hc,
                         banks=banks: t.matmul(
                        banks[0][:, 0:DH],
                        hq2[:, slot + hc * T1 + sq * 128:
                            slot + hc * T1 + sq * 128 + 128],
                        tw[:, hc * D + 0:hc * D + DH],
                        start=(hc == 0), stop=(hc == HC - 1)),
                        inc=(mm if (last and NDH == 1) else None))
                    if NDH > 1:
                        p.op("tensor", lambda t, slot=slot, sq=sq, hc=hc,
                             banks=banks: t.matmul(
                            banks[1][:, 0:DH],
                            hq2[:, slot + hc * T1 + sq * 128:
                                slot + hc * T1 + sq * 128 + 128],
                            tw[:, hc * D + DH:hc * D + 2 * DH],
                            start=(hc == 0), stop=(hc == HC - 1)),
                            inc=(mm if last else None))
                par = st % 2
                sto = dO if par == 0 else dOb
                dep = st_cnt2.get(st - 2, dO_bases2[par])
                ev = emit_epilogue(p, row0, banks[0], banks[1], mm.n, eb2,
                                   "ffn", par, dep_store=dep, sto=sto,
                                   ldsem=ldsem)
                ev_h1_list.append(ev)
                st_cnt2[st] = sto.n
            mm_after_f2[i] = mm.n
            if i + NHQS < NT1:
                load_hq(i + NHQS)
        p.run()

    p = PB(nc, "END")
    p.wait("sync", dS, dS.n)
    p.wait("sync", dSb, dSb.n)
    p.wait("sync", dX, dX.n)
    p.wait("sync", dXb, dXb.n)
    p.wait("scalar", dO, dO.n)
    p.wait("scalar", dOb, dOb.n)
    p.wait("gpsimd", dG, dG.n)
    p.run()

    stack.close()
    return nc


def host_inputs(cfg, core, x, w_qkv, w_fc, b_fc, gamma, beta, w1, conv_w,
                conv_b, w2):
    S, D, NSEQ = cfg.S, cfg.D, cfg.NSEQ
    b0 = core * NSEQ
    xs = x[b0:b0 + NSEQ]
    return {
        "x_tm": np.ascontiguousarray(xs.reshape(NSEQ * S, D)),
        "x_fm": np.ascontiguousarray(xs.transpose(0, 2, 1)),
        "wqT": np.ascontiguousarray(w_qkv[0:D].T),
        "wkT": np.ascontiguousarray(w_qkv[D:2 * D].T),
        "wvT": np.ascontiguousarray(w_qkv[2 * D:3 * D].T),
        "wfcT": np.ascontiguousarray(w_fc.T),
        "bfc": np.ascontiguousarray(b_fc.reshape(1, D)),
        "gamma": np.ascontiguousarray(gamma.reshape(1, D)),
        "beta": np.ascontiguousarray(beta.reshape(1, D)),
        "w1T": np.ascontiguousarray(w1.T),
        "w2T": np.ascontiguousarray(w2.T),
        "cw": np.ascontiguousarray(conv_w.reshape(cfg.H, 3)),
        "cb": np.ascontiguousarray(conv_b.reshape(cfg.H, 1)),
    }


def ref_forward(cfg, x, w_qkv, w_fc, b_fc, gamma, beta, w1, conv_w, conv_b,
                w2, dtype=np.float64):
    S, D, H = cfg.S, cfg.D, cfg.H
    f = lambda a: np.asarray(a, dtype=dtype)
    x, w_qkv, w_fc, b_fc = f(x), f(w_qkv), f(w_fc), f(b_fc)
    gamma, beta, w1, conv_w, conv_b, w2 = (f(gamma), f(beta), f(w1),
                                           f(conv_w), f(conv_b), f(w2))

    def ln(h):
        mu = h.mean(-1, keepdims=True)
        va = h.var(-1, keepdims=True)
        return (h - mu) / np.sqrt(va + cfg.LN_EPS)

    qkv = np.einsum("bsd,ed->bse", x, w_qkv)
    q, k, v = np.split(qkv, 3, axis=-1)
    s = np.einsum("bqd,bkd->bqk", q, k) * cfg.QSCALE
    s = s - s.max(-1, keepdims=True)
    a = np.exp(s)
    a /= a.sum(-1, keepdims=True)
    y = a.sum(-1, keepdims=True) * v.sum(1, keepdims=True)
    y = np.einsum("bqd,ed->bqe", y, w_fc) + b_fc
    h = x + y
    attn = ln(h) * gamma + beta
    x1 = ln(x + attn)

    def act_quant(t):
        sc = 127.0 / np.clip(np.abs(t).max(-1, keepdims=True), 1e-5, None)
        q = np.clip(np.round(t * sc), -128, 127)
        return q / sc

    def weight_quant(w):
        sc = 1.0 / np.clip(np.abs(w).mean(), 1e-5, None)
        return np.clip(np.round(w * sc), -1, 1) / sc

    hq = np.einsum("bsd,hd->bsh", act_quant(x1), weight_quant(w1))
    cwk = conv_w.reshape(H, 3)
    hp = np.pad(hq, ((0, 0), (1, 1), (0, 0)))
    conv = (hp[:, :-2, :] * cwk[:, 0] + hp[:, 1:-1, :] * cwk[:, 1] +
            hp[:, 2:, :] * cwk[:, 2]) + conv_b.reshape(H)
    sl = conv / (1.0 + np.exp(-conv))
    out = np.einsum("bsh,eh->bse", act_quant(sl), weight_quant(w2))
    x2 = ln(x1 + out)
    return x2


from concourse.bass_utils import run_bass_kernel_spmd

_N_CORES = 8


def kernel(L0_attn_x, L0_attn_w_qkv, L0_attn_w_fc, L0_attn_b_fc, L0_attn_gamma,
           L0_attn_beta, L0_ffn_w1, L0_ffn_conv_w, L0_ffn_conv_b, L0_ffn_w2):
    cfg = Cfg(S=2048, D=1024, H=4096, NSEQ=2, T1=256, T2=512)
    x = np.asarray(L0_attn_x)
    nc = build_program(cfg)
    in_maps = [
        host_inputs(cfg, c, x, np.asarray(L0_attn_w_qkv),
                    np.asarray(L0_attn_w_fc), np.asarray(L0_attn_b_fc),
                    np.asarray(L0_attn_gamma), np.asarray(L0_attn_beta),
                    np.asarray(L0_ffn_w1), np.asarray(L0_ffn_conv_w),
                    np.asarray(L0_ffn_conv_b), np.asarray(L0_ffn_w2))
        for c in range(_N_CORES)
    ]
    res = run_bass_kernel_spmd(nc, in_maps, core_ids=list(range(_N_CORES)))
    parts = [np.asarray(r["y"]).reshape(cfg.NSEQ, cfg.S, cfg.D)
             for r in res.results]
    return np.ascontiguousarray(np.concatenate(parts, axis=0), dtype=np.float32)


# version 21
# speedup vs baseline: 1.0486x; 1.0127x over previous
import numpy as np
import ml_dtypes
from contextlib import ExitStack

import concourse.bass as bass
import concourse.mybir as mybir

F32 = mybir.dt.float32
BF16 = mybir.dt.bfloat16
I32 = mybir.dt.int32
AF = mybir.ActivationFunctionType
ALU = mybir.AluOpType
AX = mybir.AxisListType


class Cfg:
    def __init__(self, S=2048, D=1024, H=4096, NSEQ=2, T1=256, T2=512):
        self.S, self.D, self.H, self.NSEQ = S, D, H, NSEQ
        self.DC, self.HC = D // 128, H // 128
        self.T1, self.T2 = T1, T2
        self.G = min(512, S)
        self.KBW = min(512, S)
        self.KB = S // self.KBW
        self.T = NSEQ * S
        self.ST = S // 128
        self.NGRP = S // self.G
        self.NQT = self.G // 128
        self.LN_EPS = 1e-5
        self.QSCALE = float(D) ** (-0.5)


class Sem:
    def __init__(self, h):
        self.h, self.n = h, 0


class PB:
    ENGS = ("sync", "scalar", "vector", "tensor", "gpsimd")

    def __init__(self, nc, name=""):
        self.nc, self.name = nc, name
        self.items = {e: [] for e in self.ENGS}

    def op(self, eng, fn, inc=None, k=1, drain=None):
        if drain is None:
            drain = eng in ("scalar", "vector", "gpsimd")
        tgt = None
        if inc is not None:
            inc.n += k
            tgt = (inc.h, k)

        def cl(e, fn=fn, tgt=tgt, drain=drain):
            ins = fn(e)
            if tgt is not None:
                ins.then_inc(*tgt)
            if drain:
                e.drain()

        self.items[eng].append(cl)

    def wait(self, eng, sem, val):
        self.items[eng].append(lambda e, h=sem.h, v=val: e.wait_ge(h, v))

    def run(self):
        with self.nc.Block() as block:
            for name in self.ENGS:
                fns = self.items[name]
                if not fns:
                    continue

                def body(e, fns=fns):
                    for f in fns:
                        f(e)

                getattr(block, name)(body)


def _ap(handle, offset, dims):
    return bass.AP(handle, offset, dims)


def build_program(cfg: Cfg, debug=False):
    nc = bass.Bass("TRN2", target_bir_lowering=False)
    S, D, H, DC, HC = cfg.S, cfg.D, cfg.H, cfg.DC, cfg.HC
    T, G, KB, KBW = cfg.T, cfg.G, cfg.KB, cfg.KBW
    T1, T2, ST = cfg.T1, cfg.T2, cfg.ST
    NGRP, NQT, NSEQ = cfg.NGRP, cfg.NQT, cfg.NSEQ
    NT1, NT2 = T // T1, T // T2
    EPS = cfg.LN_EPS
    DH = min(512, D)
    NDH = D // DH
    NPTB = (ST + 3) // 4

    def din(name, shape):
        return nc.dram_tensor(name, list(shape), F32, kind="ExternalInput").ap()

    x_tm = din("x_tm", (T, D))
    x_fm = din("x_fm", (NSEQ, D, S))
    wqT = din("wqT", (D, D))
    wkT = din("wkT", (D, D))
    wvT = din("wvT", (D, D))
    wfcT = din("wfcT", (D, D))
    bfc_d = din("bfc", (1, D))
    gamma_d = din("gamma", (1, D))
    beta_d = din("beta", (1, D))
    w1T = din("w1T", (D, H))
    w2T = din("w2T", (H, D))
    cw_d = din("cw", (H, 3))
    cb_d = din("cb", (H, 1))

    y_out = nc.dram_tensor("y", [T, D], F32, kind="ExternalOutput").ap()
    ik = "ExternalOutput" if debug else "Internal"
    x1s = nc.dram_tensor("x1s", [T, D], F32, kind=ik).ap()
    xqs = nc.dram_tensor("xqs", [T, D], BF16, kind=ik).ap()
    hqs = nc.dram_tensor("hqs", [NT1, HC, 128, T1], BF16, kind=ik).ap()
    f1s = nc.dram_tensor("f1s", [T, 1], F32, kind=ik).ap()
    vrow_d = nc.dram_tensor("vrow_d", [1, D], F32, kind="Internal").ap()
    if debug:
        dbg_u = nc.dram_tensor("dbg_u", [H, T], F32, kind="ExternalOutput").ap()
        dbg_mx = nc.dram_tensor("dbg_mx", [T, 2], F32, kind="ExternalOutput").ap()
        dbg_z = nc.dram_tensor("dbg_z", [H, T], F32, kind="ExternalOutput").ap()
        dbg_i = nc.dram_tensor("dbg_i", [H, T], I32, kind="ExternalOutput").ap()

    ident_d = nc.inline_tensor(np.eye(128, dtype=ml_dtypes.bfloat16), "c_ident")
    identf_d = nc.inline_tensor(np.eye(128, dtype=np.float32), "c_identf")
    esel_np = np.zeros((128, 128), dtype=np.float32)
    esel_np[0, :] = 1.0
    esel_d = nc.inline_tensor(esel_np, "c_esel")
    ones_d = nc.inline_tensor(np.ones((128, 1), dtype=np.float32), "c_ones")
    c128_d = nc.inline_tensor(np.full((128, 1), 128.0, dtype=np.float32),
                              "c_c128")

    stack = ExitStack()
    E = stack.enter_context

    def sb(name, shape, dtype):
        return E(nc.sbuf_tensor(name, list(shape), dtype))

    ident = sb("ident", (128, 128), BF16)
    identf = sb("identf", (128, 128), F32)
    esel = sb("esel", (128, 128), F32)
    onescol = sb("onescol", (128, 1), F32)
    c128 = sb("c128", (128, 1), F32)
    gammaBC = sb("gammaBC", (128, D), F32)
    betaBC = sb("betaBC", (128, D), F32)
    bfcBC = sb("bfcBC", (128, D), F32)
    rowbuf = sb("rowbuf", (128, 512), F32)
    cwcol = sb("cwcol", (128, 3 * HC), F32)
    cbcol = sb("cbcol", (128, HC), F32)
    f2cols = sb("f2cols", (128, T // 128), F32)
    wsmat = sb("wsmat", (128, 16), F32)
    wqcols = sb("wqcols", (128, 8), F32)
    cols = sb("cols", (128, 48), F32)
    colsb = sb("colsb", (128, 48), F32)
    wbuf = sb("wbuf", (128, DC * D), BF16)

    ps_s = [E(nc.psum_tensor(f"ps_s{i}", [128, 512], F32)) for i in range(4)]
    ps_a = E(nc.psum_tensor("ps_a", [128, 512], F32))
    ps_b = E(nc.psum_tensor("ps_b", [128, 512], F32))
    ps_c = E(nc.psum_tensor("ps_c", [128, 512], F32))
    ps_d = E(nc.psum_tensor("ps_d", [128, 512], F32))

    dS = Sem(E(nc.semaphore("dS")))
    dSb = Sem(E(nc.semaphore("dSb")))
    dOb = Sem(E(nc.semaphore("dOb")))
    dX = Sem(E(nc.semaphore("dX")))
    dXb = Sem(E(nc.semaphore("dXb")))
    dG = Sem(E(nc.semaphore("dG")))
    dO = Sem(E(nc.semaphore("dO")))
    mm = Sem(E(nc.semaphore("mm")))
    vv = Sem(E(nc.semaphore("vv")))
    ssm = Sem(E(nc.semaphore("ssm")))
    gg = Sem(E(nc.semaphore("gg")))

    p = PB(nc, "P0")
    for dst, srcd in ((ident, ident_d), (identf, identf_d), (esel, esel_d),
                      (onescol, ones_d), (c128, c128_d)):
        p.op("sync", lambda s, dst=dst, srcd=srcd: s.dma_start(
            out=dst[:], in_=srcd.ap()), inc=dS, k=16)
    p.op("sync", lambda s: s.dma_start(
        out=gammaBC[:], in_=_ap(gamma_d.tensor, 0, [[0, 128], [1, D]])),
        inc=dS, k=16)
    p.op("sync", lambda s: s.dma_start(
        out=betaBC[:], in_=_ap(beta_d.tensor, 0, [[0, 128], [1, D]])),
        inc=dS, k=16)
    p.op("sync", lambda s: s.dma_start(
        out=bfcBC[:], in_=_ap(bfc_d.tensor, 0, [[0, 128], [1, D]])),
        inc=dS, k=16)
    for c in range(HC):
        p.op("sync", lambda s, c=c: s.dma_start(
            out=cwcol[:, 3 * c:3 * c + 3], in_=cw_d[c * 128:(c + 1) * 128, :]),
            inc=dS, k=16)
    for c in range(HC):
        p.op("sync", lambda s, c=c: s.dma_start(
            out=cbcol[:, c:c + 1], in_=cb_d[c * 128:(c + 1) * 128, :]),
            inc=dS, k=16)
    p.wait("sync", dS, dS.n)
    p.op("vector", lambda v: v.memset(rowbuf[:], 0.0))
    p.op("vector", lambda v: v.memset(wsmat[:], 0.0))
    p.run()

    with nc.sbuf_tensor("wstage", [128, H], F32) as wstage, \
         nc.sbuf_tensor("wstage2", [128, H], F32) as wstage2, \
         nc.sbuf_tensor("wabs", [128, H], F32) as wabs:
        p = PB(nc, "W0")
        p.wait("sync", dS, dS.n)
        p.wait("sync", dSb, dSb.n)
        chunks = [(w1T[dc * 128:(dc + 1) * 128, :], H) for dc in range(DC)] + \
                 [(w2T[hc * 128:(hc + 1) * 128, :], D) for hc in range(HC)]
        NCH = len(chunks)
        for i, (src, width) in enumerate(chunks):
            stage = wstage if i % 2 == 0 else wstage2
            sem = dS if i % 2 == 0 else dSb
            if i >= 2:
                p.wait("sync", ssm, ssm.n - 1)
            p.op("sync", lambda s, stage=stage, src=src, width=width:
                 s.dma_start(out=stage[:, 0:width], in_=src), inc=sem, k=16)
            p.wait("scalar", sem, sem.n)
            if i >= 1:
                p.wait("scalar", vv, vv.n)
            p.op("scalar", lambda sc, stage=stage, width=width: sc.activation(
                out=wabs[:, 0:width], in_=stage[:, 0:width], func=AF.Abs), inc=ssm)
            p.wait("vector", ssm, ssm.n)
            p.op("vector", lambda v, i=i, width=width: v.reduce_sum(
                out=colsb[:, i:i + 1], in_=wabs[:, 0:width], axis=AX.X), inc=vv)
        p.op("vector", lambda v: v.reduce_sum(
            out=cols[:, 0:1], in_=colsb[:, 0:DC], axis=AX.X))
        p.op("vector", lambda v: v.reduce_sum(
            out=cols[:, 1:2], in_=colsb[:, DC:DC + HC], axis=AX.X), inc=vv)
        p.wait("tensor", vv, vv.n)
        p.op("tensor", lambda t: t.matmul(ps_c[0:1, 0:1], onescol[:], cols[:, 0:1],
                                          start=True, stop=True))
        p.op("tensor", lambda t: t.matmul(ps_c[0:1, 1:2], onescol[:], cols[:, 1:2],
                                          start=True, stop=True), inc=mm)
        p.wait("vector", mm, mm.n)
        p.op("vector", lambda v: v.tensor_copy(out=wsmat[0:1, 0:2], in_=ps_c[0:1, 0:2]))
        p.op("vector", lambda v: v.tensor_scalar(
            out=wsmat[0:1, 2:3], in0=wsmat[0:1, 0:1], scalar1=1.0 / (D * H),
            scalar2=1e-5, op0=ALU.mult, op1=ALU.max))
        p.op("vector", lambda v: v.tensor_scalar(
            out=wsmat[0:1, 3:4], in0=wsmat[0:1, 1:2], scalar1=1.0 / (D * H),
            scalar2=1e-5, op0=ALU.mult, op1=ALU.max))
        p.op("vector", lambda v: v.reciprocal(out=wsmat[0:1, 4:5], in_=wsmat[0:1, 2:3]))
        p.op("vector", lambda v: v.reciprocal(out=wsmat[0:1, 5:6], in_=wsmat[0:1, 3:4]))
        p.op("vector", lambda v: v.reciprocal(out=wsmat[0:1, 6:7], in_=wsmat[0:1, 4:5]))
        p.op("vector", lambda v: v.reciprocal(out=wsmat[0:1, 7:8], in_=wsmat[0:1, 5:6]))
        p.op("vector", lambda v: v.tensor_scalar(
            out=wsmat[0:1, 8:9], in0=wsmat[0:1, 6:7], scalar1=-1.0, scalar2=None,
            op0=ALU.mult))
        p.op("vector", lambda v: v.tensor_scalar(
            out=wsmat[0:1, 9:10], in0=wsmat[0:1, 7:8], scalar1=-1.0, scalar2=None,
            op0=ALU.mult), inc=vv)
        p.wait("tensor", vv, vv.n)
        for j, cell in enumerate((4, 5, 8, 9)):
            p.op("tensor", lambda t, j=j, cell=cell: t.matmul(
                ps_d[:, j:j + 1], esel[:], wsmat[:, cell:cell + 1],
                start=True, stop=True), inc=(mm if j == 3 else None))
        p.wait("vector", mm, mm.n)
        p.op("vector", lambda v: v.tensor_copy(out=wqcols[:, 0:4], in_=ps_d[:, 0:4]),
             inc=vv)
        p.run()

    def emit_epilogue(p, row0, psA, psB, mm_wait, eb, kind, par, dep_store=None,
                      sto=None, ldsem=None):
        xres, htile, atile, ln1t, sqsc, x1t, iqt, iqtb, xqt = eb
        if sto is None:
            sto = dO
        c0 = 24 * par
        xr = xres[:, par * D:(par + 1) * D]
        x1 = x1t[:, par * D:(par + 1) * D]
        xq = xqt[:, par * D:(par + 1) * D]

        def C(j):
            return cols[:, c0 + j:c0 + j + 1]

        p.wait("vector", mm, mm_wait)
        p.wait("vector", ldsem, ldsem.n)
        if kind == "ffn":
            f2c = f2cols[:, row0 // 128: row0 // 128 + 1]
            p.op("vector", lambda v: v.tensor_scalar(
                out=htile[:, 0:DH], in0=psA[:, 0:DH], scalar1=f2c, scalar2=None,
                op0=ALU.mult))
            if NDH > 1:
                p.op("vector", lambda v: v.tensor_scalar(
                    out=htile[:, DH:D], in0=psB[:, 0:DH], scalar1=f2c, scalar2=None,
                    op0=ALU.mult))
            p.op("vector", lambda v: v.tensor_tensor(
                out=atile[:], in0=htile[:], in1=xr, op=ALU.add), inc=vv)
            ev_h1 = vv.n
        else:
            p.op("vector", lambda v: v.tensor_tensor(
                out=htile[:], in0=atile[:], in1=bfcBC[:], op=ALU.add))
            p.op("vector", lambda v: v.tensor_tensor(
                out=atile[:], in0=htile[:], in1=xr, op=ALU.add), inc=vv)
            ev_h1 = vv.n
        p.op("vector", lambda v: v.reduce_sum(out=C(0), in_=atile[:], axis=AX.X))
        p.wait("scalar", vv, ev_h1)
        p.op("scalar", lambda sc: sc.activation(
            out=sqsc[:], in_=atile[:], func=AF.Square, accum_out=C(1)), inc=ssm)
        ss_sq1 = ssm.n
        p.wait("vector", ssm, ss_sq1)
        p.op("vector", lambda v: v.tensor_scalar(
            out=C(2), in0=C(0), scalar1=1.0 / D, scalar2=None, op0=ALU.mult))
        p.op("vector", lambda v: v.tensor_scalar(
            out=C(3), in0=C(1), scalar1=1.0 / D, scalar2=EPS, op0=ALU.mult,
            op1=ALU.add))
        p.op("vector", lambda v: v.tensor_scalar(
            out=C(4), in0=C(2), scalar1=C(2), scalar2=-1.0, op0=ALU.mult,
            op1=ALU.mult))
        p.op("vector", lambda v: v.tensor_tensor(
            out=C(5), in0=C(3), in1=C(4), op=ALU.add), inc=vv)
        ev_var1 = vv.n
        p.wait("scalar", vv, ev_var1)
        p.op("scalar", lambda sc: sc.activation(out=C(6), in_=C(5), func=AF.Ln))
        p.op("scalar", lambda sc: sc.activation(out=C(7), in_=C(6), func=AF.Exp,
                                                scale=-0.5), inc=ssm)
        ss_sd1 = ssm.n
        p.wait("vector", ssm, ss_sd1)
        if kind == "ffn":
            if dep_store is not None:
                p.wait("vector", sto, dep_store)
            p.op("vector", lambda v: v.tensor_scalar(
                out=x1, in0=atile[:], scalar1=C(2), scalar2=C(7),
                op0=ALU.subtract, op1=ALU.mult), inc=vv)
            ev_out = vv.n
            p.wait("scalar", vv, ev_out)
            p.op("scalar", lambda sc: sc.dma_start(
                out=y_out[row0:row0 + 128, :], in_=x1), inc=sto, k=16)
            return ev_h1
        p.op("vector", lambda v: v.tensor_scalar(
            out=ln1t[:], in0=atile[:], scalar1=C(2), scalar2=C(7),
            op0=ALU.subtract, op1=ALU.mult))
        p.op("vector", lambda v: v.tensor_tensor(
            out=htile[:], in0=ln1t[:], in1=gammaBC[:], op=ALU.mult))
        p.op("vector", lambda v: v.tensor_tensor(
            out=ln1t[:], in0=htile[:], in1=betaBC[:], op=ALU.add))
        p.op("vector", lambda v: v.tensor_tensor(
            out=htile[:], in0=ln1t[:], in1=xr, op=ALU.add), inc=vv)
        ev_h2 = vv.n
        p.op("vector", lambda v: v.reduce_sum(out=C(8), in_=htile[:], axis=AX.X))
        p.wait("scalar", vv, ev_h2)
        p.op("scalar", lambda sc: sc.activation(
            out=sqsc[:], in_=htile[:], func=AF.Square, accum_out=C(9)), inc=ssm)
        ss_sq2 = ssm.n
        p.wait("vector", ssm, ss_sq2)
        p.op("vector", lambda v: v.tensor_scalar(
            out=C(10), in0=C(8), scalar1=1.0 / D, scalar2=None, op0=ALU.mult))
        p.op("vector", lambda v: v.tensor_scalar(
            out=C(11), in0=C(9), scalar1=1.0 / D, scalar2=EPS, op0=ALU.mult,
            op1=ALU.add))
        p.op("vector", lambda v: v.tensor_scalar(
            out=C(12), in0=C(10), scalar1=C(10), scalar2=-1.0, op0=ALU.mult,
            op1=ALU.mult))
        p.op("vector", lambda v: v.tensor_tensor(
            out=C(13), in0=C(11), in1=C(12), op=ALU.add), inc=vv)
        ev_var2 = vv.n
        p.wait("scalar", vv, ev_var2)
        p.op("scalar", lambda sc: sc.activation(out=C(14), in_=C(13), func=AF.Ln))
        p.op("scalar", lambda sc: sc.activation(out=C(15), in_=C(14), func=AF.Exp,
                                                scale=-0.5), inc=ssm)
        ss_sd2 = ssm.n
        p.wait("vector", ssm, ss_sd2)
        if dep_store is not None:
            p.wait("vector", sto, dep_store)
        p.op("vector", lambda v: v.tensor_scalar(
            out=x1, in0=htile[:], scalar1=C(10), scalar2=C(15),
            op0=ALU.subtract, op1=ALU.mult), inc=vv)
        ev_x1 = vv.n
        p.wait("scalar", vv, ev_x1)
        p.op("scalar", lambda sc: sc.dma_start(
            out=x1s[row0:row0 + 128, :], in_=x1), inc=sto, k=16)
        p.op("vector", lambda v: v.reduce_max(
            out=C(16), in_=x1, axis=AX.X, apply_absolute_value=True))
        p.op("vector", lambda v: v.tensor_scalar(
            out=C(17), in0=C(16), scalar1=1e-5, scalar2=None, op0=ALU.max))
        p.op("vector", lambda v: v.reciprocal(out=C(18), in_=C(17)))
        p.op("vector", lambda v: v.tensor_scalar(
            out=C(19), in0=C(18), scalar1=127.0, scalar2=None, op0=ALU.mult))
        p.op("vector", lambda v: v.reciprocal(out=C(20), in_=C(19)), inc=vv)
        ev_scale = vv.n
        p.wait("scalar", vv, ev_scale)
        p.op("scalar", lambda sc: sc.dma_start(
            out=f1s[row0:row0 + 128, :], in_=C(20)), inc=sto, k=16)
        p.op("vector", lambda v: v.tensor_scalar(
            out=sqsc[:], in0=x1, scalar1=C(19), scalar2=128.0, op0=ALU.mult,
            op1=ALU.add))
        p.op("vector", lambda v: v.tensor_copy(out=iqt[:], in_=sqsc[:]))
        p.op("vector", lambda v: v.tensor_tensor(
            out=ln1t[:], in0=sqsc[:], in1=iqt[:], op=ALU.subtract))
        p.op("vector", lambda v: v.scalar_tensor_tensor(
            out=htile[:], in0=ln1t[:], scalar=0.5, in1=iqt[:], op0=ALU.is_ge,
            op1=ALU.add))
        p.op("vector", lambda v: v.tensor_scalar(
            out=xq, in0=htile[:], scalar1=-128.0, scalar2=None, op0=ALU.add),
            inc=vv)
        ev_xq = vv.n
        p.wait("scalar", vv, ev_xq)
        p.op("scalar", lambda sc: sc.dma_start(
            out=xqs[row0:row0 + 128, :], in_=xq), inc=sto, k=16)
        return ev_h2

    attn_stack = ExitStack()
    qf = attn_stack.enter_context(nc.sbuf_tensor("qf", [128, DC * S], BF16))
    kf = attn_stack.enter_context(nc.sbuf_tensor("kf", [128, DC * S], BF16))
    wrowbc = attn_stack.enter_context(nc.sbuf_tensor("wrowbc", [128, D], F32))
    vrow = attn_stack.enter_context(nc.sbuf_tensor("vrow", [1, D], F32))
    wf32 = attn_stack.enter_context(nc.sbuf_tensor("wf32", [128, 2 * D], F32))

    def wload(p, src):
        for dc in range(DC):
            p.op("gpsimd", lambda g, dc=dc, src=src: g.dma_start(
                out=wbuf[:, dc * D:(dc + 1) * D],
                in_=src[dc * 128:(dc + 1) * 128, :]), inc=dG, k=16)

    def mm_evac(p, items, emit_group, emit_evac, banks):
        nb = len(banks)
        evbase = vv.n
        mmbase = mm.n
        for i, it in enumerate(items):
            if i >= nb:
                p.wait("tensor", vv, evbase + (i - nb + 1))
            emit_group(p, it, banks[i % nb])
            assert mm.n == mmbase + i + 1
            p.wait("vector", mm, mm.n)
            emit_evac(p, it, banks[i % nb])
            assert vv.n == evbase + i + 1

    xf = attn_stack.enter_context(nc.sbuf_tensor("xfb", [128, DC * S], BF16))
    dGq = {}

    def emit_A0(p, s_i):
        for dc in range(DC):
            p.op("gpsimd", lambda g, dc=dc, s_i=s_i: g.dma_start(
                out=xf[:, dc * S:(dc + 1) * S],
                in_=x_fm[s_i, dc * 128:(dc + 1) * 128, :]), inc=dG, k=16)
        wload(p, wqT)
        dGq[s_i] = dG.n

    for s_i in range(NSEQ):
        if True:
            p = PB(nc, f"APRO_{s_i}")
            if s_i not in dGq:
                emit_A0(p, s_i)
            p.wait("gpsimd", dG, dGq[s_i])
            p.wait("tensor", dG, dGq[s_i])

            q_items = [(cb, ec) for cb in range(S // KBW) for ec in range(DC)]

            def q_group(p, it, bank):
                cb, ec = it
                for dc in range(DC):
                    p.op("tensor", lambda t, cb=cb, ec=ec, dc=dc, bank=bank:
                         t.matmul(bank[:, 0:KBW],
                                  wbuf[:, dc * D + ec * 128: dc * D + ec * 128 + 128],
                                  xf[:, dc * S + cb * KBW: dc * S + (cb + 1) * KBW],
                                  start=(dc == 0), stop=(dc == DC - 1)),
                         inc=(mm if dc == DC - 1 else None))

            def q_evac(p, it, bank):
                cb, ec = it
                p.op("vector", lambda v, cb=cb, ec=ec, bank=bank: v.tensor_scalar(
                    out=qf[:, ec * S + cb * KBW: ec * S + (cb + 1) * KBW],
                    in0=bank[:, 0:KBW], scalar1=cfg.QSCALE, scalar2=None,
                    op0=ALU.mult), inc=vv)

            mm_evac(p, q_items, q_group, q_evac, [ps_a, ps_b])

            p.wait("gpsimd", mm, mm.n)
            wload(p, wkT)
            p.wait("tensor", dG, dG.n)
            p.wait("tensor", vv, vv.n)

            def k_evac(p, it, bank):
                cb, ec = it
                p.op("vector", lambda v, cb=cb, ec=ec, bank=bank: v.tensor_copy(
                    out=kf[:, ec * S + cb * KBW: ec * S + (cb + 1) * KBW],
                    in_=bank[:, 0:KBW]), inc=vv)

            mm_evac(p, q_items, q_group, k_evac, [ps_a, ps_b])

            for dc in range(DC):
                p.op("vector", lambda v, dc=dc: v.reduce_sum(
                    out=colsb[:, 32 + dc:33 + dc], in_=xf[:, dc * S:(dc + 1) * S],
                    axis=AX.X), inc=(vv if dc == DC - 1 else None))
            p.wait("tensor", vv, vv.n)
            p.wait("sync", dS, dS.n)
            p.wait("sync", dSb, dSb.n)
            mvb = mm.n
            for dc in range(DC):
                sem = dS if dc % 2 == 0 else dSb
                if dc >= 2:
                    p.wait("sync", mm, mvb + NDH * (dc - 1))
                    p.wait("sync", sem, sem.n)
                p.op("sync", lambda s, dc=dc: s.dma_start(
                    out=wf32[:, (dc % 2) * D:(dc % 2 + 1) * D],
                    in_=wvT[dc * 128:(dc + 1) * 128, :]), inc=sem, k=16)
                p.wait("tensor", sem, sem.n)
                for h in range(NDH):
                    bank = ps_c if h == 0 else ps_d
                    p.op("tensor", lambda t, h=h, dc=dc, bank=bank: t.matmul(
                        bank[0:1, 0:DH], colsb[:, 32 + dc:33 + dc],
                        wf32[:, (dc % 2) * D + h * DH: (dc % 2) * D + (h + 1) * DH],
                        start=(dc == 0), stop=(dc == DC - 1)), inc=mm)
            p.wait("vector", mm, mm.n)
            for h in range(NDH):
                bank = ps_c if h == 0 else ps_d
                p.op("vector", lambda v, h=h, bank=bank: v.tensor_copy(
                    out=vrow[0:1, h * DH:(h + 1) * DH], in_=bank[0:1, 0:DH]),
                    inc=(vv if h == NDH - 1 else None))
            p.wait("sync", vv, vv.n)
            p.op("sync", lambda s: s.dma_start(out=vrow_d, in_=vrow[0:1, :]),
                 inc=dS, k=16)
            p.wait("sync", dS, dS.n)
            for dc in range(DC):
                p.op("sync", lambda s, dc=dc: s.dma_start(
                    out=colsb[:, 40 + dc:41 + dc],
                    in_=_ap(vrow_d.tensor, dc * 128, [[1, 128], [1, 1]])),
                    inc=dS, k=16)

        p.wait("sync", dS, dS.n)
        p.wait("sync", dSb, dSb.n)
        p.wait("tensor", dS, dS.n)
        p.wait("tensor", dSb, dSb.n)
        mvb = mm.n
        for dc in range(DC):
            sem = dS if dc % 2 == 0 else dSb
            if dc >= 2:
                p.wait("sync", mm, mvb + NDH * (dc - 1))
                p.wait("sync", sem, sem.n)
            p.op("sync", lambda s, dc=dc: s.dma_start(
                out=wf32[:, (dc % 2) * D:(dc % 2 + 1) * D],
                in_=wfcT[dc * 128:(dc + 1) * 128, :]), inc=sem, k=16)
            p.wait("tensor", sem, sem.n)
            for h in range(NDH):
                bank = ps_c if h == 0 else ps_d
                p.op("tensor", lambda t, h=h, dc=dc, bank=bank: t.matmul(
                    bank[0:1, 0:DH], colsb[:, 40 + dc:41 + dc],
                    wf32[:, (dc % 2) * D + h * DH: (dc % 2) * D + (h + 1) * DH],
                    start=(dc == 0), stop=(dc == DC - 1)), inc=mm)
        for h in range(NDH):
            bank = ps_c if h == 0 else ps_d
            p.wait("vector", mm, mm.n - (NDH - 1 - h))
            p.op("vector", lambda v, h=h, bank=bank: v.tensor_copy(
                out=rowbuf[0:1, 0:DH], in_=bank[0:1, 0:DH]), inc=vv)
            ev_row = vv.n
            p.wait("tensor", vv, ev_row)
            p.op("tensor", lambda t, h=h: t.matmul(
                ps_s[0][:, 0:DH], esel[:], rowbuf[:, 0:DH],
                start=True, stop=True), inc=mm)
            p.wait("vector", mm, mm.n)
            p.op("vector", lambda v, h=h: v.tensor_copy(
                out=wrowbc[:, h * DH:(h + 1) * DH], in_=ps_s[0][:, 0:DH]),
                inc=vv)
        p.run()

        a3s = ExitStack()
        Ea = a3s.enter_context
        disc = Ea(nc.sbuf_tensor(f"disc{s_i}", [128, KBW], F32))
        xres = Ea(nc.sbuf_tensor(f"xres{s_i}", [128, 2 * D], F32))
        htile = Ea(nc.sbuf_tensor(f"htile{s_i}", [128, D], F32))
        atile = Ea(nc.sbuf_tensor(f"atile{s_i}", [128, D], F32))
        ln1t = Ea(nc.sbuf_tensor(f"ln1t{s_i}", [128, D], F32))
        sqsc = Ea(nc.sbuf_tensor(f"sqsc{s_i}", [128, D], F32))
        x1t = Ea(nc.sbuf_tensor(f"x1t{s_i}", [128, 2 * D], F32))
        iqt = Ea(nc.sbuf_tensor(f"iqt{s_i}", [128, D], I32))
        iqtb = Ea(nc.sbuf_tensor(f"iqtb{s_i}", [128, D], I32))
        xqt = Ea(nc.sbuf_tensor(f"xqt{s_i}", [128, 2 * D], BF16))
        eb = (xres, htile, atile, ln1t, sqsc, x1t, iqt, iqtb, xqt)
        if True:
            if True:
                p = PB(nc, f"A3_{s_i}")
                if s_i + 1 < NSEQ:
                    emit_A0(p, s_i + 1)
                dO_bases = {0: dO.n, 1: dOb.n}
                exp_ev = {}
                mm_at = {}
                epi_q = {}
                for qt in range(NGRP * NQT):
                    row0 = s_i * S + qt * 128
                    qcol = qt * 128
                    ldsem = dX if qt % 2 == 0 else dXb
                    if qt >= 2:
                        p.wait("sync", vv, ev_h2_list[qt - 2])
                    p.op("sync", lambda s, row0=row0, qt=qt: s.dma_start(
                        out=xres[:, (qt % 2) * D:(qt % 2 + 1) * D],
                        in_=x_tm[row0:row0 + 128, :]), inc=ldsem, k=16)
                    pset = ((ps_s[0], ps_s[1], ps_s[2], ps_s[3])
                            if qt % 2 == 0 else (ps_a, ps_b, ps_c, ps_d))
                    if qt > 1:
                        p.wait("tensor", ssm, exp_ev[qt - 2])
                    for kb in range(KB):
                        for dc in range(DC):
                            p.op("tensor", lambda t, qcol=qcol, kb=kb, dc=dc,
                                 pset=pset: t.matmul(
                                     pset[kb][:, 0:KBW],
                                     qf[:, dc * S + qcol: dc * S + qcol + 128],
                                     kf[:, dc * S + kb * KBW: dc * S + (kb + 1) * KBW],
                                     start=(dc == 0), stop=(dc == DC - 1)),
                                 inc=(mm if kb == KB - 1 and dc == DC - 1 else None))
                    mm_at[qt] = mm.n
                    p.wait("vector", mm, mm.n)
                    for kb in range(KB):
                        p.op("vector", lambda v, kb=kb, pset=pset: v.reduce_max(
                            out=colsb[:, kb:kb + 1], in_=pset[kb][:, 0:KBW],
                            axis=AX.X, negate=True))
                    p.op("vector", lambda v: v.tensor_reduce(
                        out=colsb[:, 8:9], in_=colsb[:, 0:KB], axis=AX.X,
                        op=ALU.min), inc=vv)
                    ev_negm = vv.n
                    p.wait("scalar", vv, ev_negm)
                    for kb in range(KB):
                        p.op("scalar", lambda sc, kb=kb, pset=pset: sc.activation(
                            out=disc[:], in_=pset[kb][:, 0:KBW], func=AF.Exp,
                            bias=colsb[:, 8:9], scale=1.0,
                            accum_out=colsb[:, 16 + kb:17 + kb]),
                            inc=(ssm if kb == KB - 1 else None))
                    exp_ev[qt] = ssm.n
                    if qt >= 1:
                        row0p, parp, stop, depp, ldp = epi_q[qt - 1]
                        ev = emit_epilogue(p, row0p, None, None, mm_at[qt - 1],
                                           eb, "attn", parp, dep_store=depp,
                                           sto=stop, ldsem=ldp)
                        ev_h2_list.append(ev)
                    p.wait("vector", ssm, exp_ev[qt])
                    p.op("vector", lambda v: v.reduce_sum(
                        out=colsb[:, 24:25], in_=colsb[:, 16:16 + KB], axis=AX.X))
                    p.op("vector", lambda v: v.reciprocal(
                        out=colsb[:, 25:26], in_=colsb[:, 24:25]))
                    p.op("vector", lambda v: v.tensor_scalar(
                        out=colsb[:, 26:27], in0=colsb[:, 24:25],
                        scalar1=colsb[:, 25:26], scalar2=None, op0=ALU.mult))
                    p.op("vector", lambda v: v.tensor_scalar(
                        out=atile[:], in0=wrowbc[:], scalar1=colsb[:, 26:27],
                        scalar2=None, op0=ALU.mult), inc=vv)
                    par = qt % 2
                    sto = dO if par == 0 else dOb
                    dep = dO_bases[par] + (48 * (qt // 2) if qt >= 2 else 0)
                    if qt == 0:
                        ev_h2_list = []
                    epi_q[qt] = (row0, par, sto, dep, ldsem)
                qt_last = NGRP * NQT - 1
                row0p, parp, stop, depp, ldp = epi_q[qt_last]
                ev = emit_epilogue(p, row0p, None, None, mm_at[qt_last], eb,
                                   "attn", parp, dep_store=depp, sto=stop,
                                   ldsem=ldp)
                ev_h2_list.append(ev)
                p.run()
        p = PB(nc, f"A3drain_{s_i}")
        for eng in ("sync", "scalar", "vector", "tensor", "gpsimd"):
            p.wait(eng, dO, dO.n)
            p.wait(eng, dOb, dOb.n)
            p.wait(eng, dX, dX.n)
            p.wait(eng, dXb, dXb.n)
        p.run()
        a3s.close()

    attn_stack.close()

    tw = stack.enter_context(nc.sbuf_tensor("tw", [128, DC * H], BF16))

    def quantize_weights(srcT, nch, width, dst_off):
        with nc.sbuf_tensor(f"qstage_{dst_off}", [128, width], F32) as qstage, \
             nc.sbuf_tensor(f"qstage2_{dst_off}", [128, width], F32) as qstage2, \
             nc.sbuf_tensor(f"qz_{dst_off}", [128, width], F32) as qz, \
             nc.sbuf_tensor(f"qa_{dst_off}", [128, width], F32) as qa:
            p = PB(nc, "W1")
            p.wait("sync", dS, dS.n)
            p.wait("sync", dSb, dSb.n)
            if debug:
                for eng in ("sync", "vector"):
                    p.wait(eng, dOb, dOb.n)
                    p.wait(eng, dO, dO.n)
            scol = wqcols[:, 0:1] if dst_off == 0 else wqcols[:, 1:2]
            for c in range(nch):
                stage = qstage if c % 2 == 0 else qstage2
                sem = dS if c % 2 == 0 else dSb
                if c >= 2:
                    p.wait("sync", vv, vv.n - 3)
                p.op("sync", lambda s, c=c, stage=stage: s.dma_start(
                    out=stage[:, 0:width], in_=srcT[c * 128:(c + 1) * 128, :]),
                    inc=sem, k=16)
                p.wait("vector", sem, sem.n)
                p.op("vector", lambda v, stage=stage: v.tensor_scalar(
                    out=qz[:, 0:width], in0=stage[:, 0:width], scalar1=scol,
                    scalar2=None, op0=ALU.mult), inc=vv)
                p.op("vector", lambda v: v.tensor_scalar(
                    out=qa[:, 0:width], in0=qz[:, 0:width], scalar1=0.5,
                    scalar2=None, op0=ALU.is_ge), inc=vv)
                p.op("vector", lambda v, c=c: v.scalar_tensor_tensor(
                    out=tw[:, c * width:(c + 1) * width], in0=qz[:, 0:width],
                    scalar=-0.5, in1=qa[:, 0:width], op0=ALU.is_le,
                    op1=ALU.subtract), inc=vv)
            p.run()

    quantize_weights(w1T, DC, H, 0)

    T1W2 = T1 + 2
    NSB1 = T1 // 128
    NQG = min(8, HC)
    WW = T1 + 32
    with nc.sbuf_tensor("xqF", [128, 2 * DC * WW], BF16) as xqF, \
         nc.sbuf_tensor("ubuf", [128, HC * T1W2], F32) as ubuf, \
         nc.sbuf_tensor("vt1", [128, 4 * T1W2], F32) as vt1, \
         nc.sbuf_tensor("vt2", [128, 2 * T1W2], F32) as vt2, \
         nc.sbuf_tensor("maxM", [128, T1], F32) as maxM, \
         nc.sbuf_tensor("rbuf", [128, NQG * T1], BF16) as rbuf, \
         nc.sbuf_tensor("f1bc", [128, 4 * T1W2], F32) as f1bc, \
         nc.sbuf_tensor("s2bc8", [128, NQG * T1], F32) as s2bc8, \
         nc.sbuf_tensor("zscr", [128, NQG * T1], F32) as zscr, \
         nc.sbuf_tensor("iqq", [128, NQG * T1], I32) as iqq, \
         nc.sbuf_tensor("dsb", [128, NQG * T1], BF16) as dsb, \
         nc.sbuf_tensor("accb", [128, NQG * T1], BF16) as accb, \
         nc.sbuf_tensor("hqst", [128, HC * T1], BF16) as hqst:
        p = PB(nc, "F1")
        p.wait("sync", dO, dO.n)
        p.wait("sync", dOb, dOb.n)
        p.wait("sync", dX, dX.n)
        p.wait("sync", dXb, dXb.n)
        p.wait("sync", dS, dS.n)

        def tgeom(i):
            s_i = (i * T1) // S
            tb = i * T1
            lo = max(tb - 1, s_i * S)
            hi = min(tb + T1 + 1, (s_i + 1) * S)
            return tb, lo, hi, hi - lo, lo - (tb - 1), min(max(tb - 16, 0), T - WW)

        ldsems = (dX, dXb)
        stsems = (dO, dOb)
        mm_after_u = {}
        lad = {}
        ev_u = {}
        ev_conv = {}
        ss_vt1 = {}
        ss_silu = {}
        st_cnt = {}
        ld_cnt = {}

        def emit_loads(i):
            tb, lo, hi, W, off, w0 = tgeom(i)
            par = i % 2
            sem = ldsems[par]
            if i >= 2:
                p.wait("sync", mm, mm_after_u[i - 2])
                p.wait("sync", sem, sem.n)
            for dc in range(DC):
                p.op("sync", lambda s, dc=dc, w0=w0, par=par: s.dma_start(
                    out=xqF[:, (par * DC + dc) * WW:(par * DC + dc + 1) * WW],
                    in_=xqs[w0:w0 + WW, dc * 128:(dc + 1) * 128], transpose=True),
                    inc=sem, k=16)
            p.op("sync", lambda s, lo=lo, W=W, par=par: s.dma_start(
                out=f1bc[:, 2 * par * T1W2:2 * par * T1W2 + W],
                in_=_ap(f1s.tensor, lo, [[0, 128], [1, W]])), inc=sem, k=16)
            ld_cnt[i] = sem.n

        def emit_conv_tail(i, hc):
            cq = hc % 4
            cpar = hc % 2
            p.wait("vector", ssm, ss_vt1[(i, hc)])
            p.op("vector", lambda v, hc=hc, cq=cq, cpar=cpar:
                 v.scalar_tensor_tensor(
                out=vt2[:, cpar * T1W2:cpar * T1W2 + T1],
                in0=ubuf[:, hc * T1W2:hc * T1W2 + T1],
                scalar=cwcol[:, 3 * hc:3 * hc + 1],
                in1=vt1[:, cq * T1W2:cq * T1W2 + T1],
                op0=ALU.mult, op1=ALU.add))
            p.op("vector", lambda v, hc=hc, cq=cq, cpar=cpar:
                 v.scalar_tensor_tensor(
                out=vt1[:, cq * T1W2:cq * T1W2 + T1],
                in0=ubuf[:, hc * T1W2 + 2:hc * T1W2 + 2 + T1],
                scalar=cwcol[:, 3 * hc + 2:3 * hc + 3],
                in1=vt2[:, cpar * T1W2:cpar * T1W2 + T1],
                op0=ALU.mult, op1=ALU.add), inc=vv)
            ev_conv[(i, hc)] = vv.n
            p.wait("scalar", vv, ev_conv[(i, hc)])
            p.op("scalar", lambda sc, hc=hc, cq=cq: sc.activation(
                out=ubuf[:, hc * T1W2 + 1:hc * T1W2 + 1 + T1],
                in_=vt1[:, cq * T1W2:cq * T1W2 + T1], func=AF.Silu),
                inc=ssm)
            ss_silu[(i, hc)] = ssm.n

        qg_iqq = {}

        def emit_quant_A(i, g0):
            ng = min(NQG, HC - g0)
            FD = ng * T1
            s28 = s2bc8[:, 0:FD]
            p.op("vector", lambda v, g0=g0, ng=ng, FD=FD, s28=s28: v.tensor_tensor(
                out=zscr[:, 0:FD],
                in0=_ap(ubuf, g0 * T1W2 + 1,
                        [[HC * T1W2, 128], [T1W2, ng], [1, T1]]),
                in1=s28, op=ALU.mult), inc=vv)
            ev_z = vv.n
            p.wait("scalar", vv, ev_z)
            if g0 // NQG >= 1:
                p.wait("scalar", vv, lad[(i, g0 // NQG - 1)])
            elif i >= 1:
                p.wait("scalar", vv, lad[(i - 1, (HC - 1) // NQG)])
            p.op("scalar", lambda sc, FD=FD: sc.activation(
                out=iqq[:, 0:FD], in_=zscr[:, 0:FD], func=AF.Identity,
                bias=c128[:, 0:1], scale=1.0), inc=ssm)
            qg_iqq[(i, g0 // NQG)] = ssm.n

        def emit_quant_B(i, g0):
            ng = min(NQG, HC - g0)
            FD = ng * T1
            if g0 == 0:
                if i >= 1:
                    p.wait("vector", stsems[(i - 1) % 2], st_cnt[i - 1])
            p.wait("vector", ssm, qg_iqq[(i, g0 // NQG)])
            p.op("vector", lambda v, FD=FD: v.scalar_tensor_tensor(
                out=dsb[:, 0:FD], in0=zscr[:, 0:FD], scalar=128.0,
                in1=iqq[:, 0:FD], op0=ALU.add, op1=ALU.subtract))
            p.op("vector", lambda v, FD=FD: v.tensor_scalar(
                out=accb[:, 0:FD], in0=dsb[:, 0:FD], scalar1=-1.5,
                scalar2=-130.0, op0=ALU.is_ge, op1=ALU.add))
            for thr in (-0.5, 0.5, 1.5, 2.5):
                p.op("vector", lambda v, FD=FD, thr=thr: v.tensor_scalar(
                    out=rbuf[:, 0:FD], in0=dsb[:, 0:FD], scalar1=thr,
                    scalar2=None, op0=ALU.is_ge))
                p.op("vector", lambda v, FD=FD: v.tensor_tensor(
                    out=accb[:, 0:FD], in0=accb[:, 0:FD], in1=rbuf[:, 0:FD],
                    op=ALU.add))
            p.op("vector", lambda v, g0=g0, FD=FD: v.tensor_tensor(
                out=hqst[:, g0 * T1:g0 * T1 + FD], in0=accb[:, 0:FD],
                in1=iqq[:, 0:FD], op=ALU.add), inc=vv)
            lad[(i, g0 // NQG)] = vv.n

        def emit_store(i):
            par = i % 2
            sem = stsems[par]
            p.wait("sync", vv, lad[(i, (HC - 1) // NQG)])
            p.wait("sync", sem, sem.n)
            p.op("sync", lambda s, i=i: s.dma_start(
                out=_ap(hqs.tensor, i * HC * 128 * T1,
                        [[T1, 128], [128 * T1, HC], [1, T1]]),
                in_=_ap(hqst, 0, [[HC * T1, 128], [T1, HC], [1, T1]])),
                inc=sem, k=16)
            st_cnt[i] = sem.n

        emit_loads(0)
        if NT1 > 1:
            emit_loads(1)
        for i in range(NT1):
            tb, lo, hi, W, off, w0 = tgeom(i)
            par = i % 2
            fb_raw = 2 * par * T1W2
            fb_s = fb_raw + T1W2
            p.wait("vector", ldsems[par], ld_cnt[i])
            p.op("vector", lambda v, fb_s=fb_s: v.memset(
                f1bc[:, fb_s:fb_s + T1W2], 0.0))
            p.op("vector", lambda v, fb_raw=fb_raw, fb_s=fb_s, off=off, W=W:
                 v.tensor_scalar(
                     out=f1bc[:, fb_s + off:fb_s + off + W],
                     in0=f1bc[:, fb_raw:fb_raw + W],
                     scalar1=wqcols[:, 2:3], scalar2=None, op0=ALU.mult), inc=vv)
            ev_f1bc = vv.n
            p.wait("tensor", vv, ev_f1bc)
            p.wait("tensor", ldsems[par], ld_cnt[i])
            for hp in range(HC // 2):
                hc0 = 2 * hp
                if i >= 1 and hc0 % NQG == 0:
                    emit_quant_A(i - 1, hc0)
                if i >= 1 and hc0 % NQG == NQG // 2:
                    emit_quant_B(i - 1, hc0 - NQG // 2)
                for hc in (hc0, hc0 + 1):
                    bank = (ps_a, ps_b, ps_s[0], ps_s[1])[hc % 4]
                    if hc >= 4:
                        p.wait("tensor", vv, ev_u[(i, hc - 4)])
                    elif i >= 1:
                        p.wait("tensor", vv, ev_u[(i - 1, HC - 4 + hc)])
                    for dc in range(DC):
                        p.op("tensor", lambda t, hc=hc, dc=dc, bank=bank,
                             par=par, lo=lo, hi=hi, w0=w0, W=W: t.matmul(
                            bank[:, 0:W],
                            tw[:, dc * H + hc * 128: dc * H + hc * 128 + 128],
                            xqF[:, (par * DC + dc) * WW + (lo - w0):
                                (par * DC + dc) * WW + (hi - w0)],
                            start=(dc == 0), stop=(dc == DC - 1)),
                            inc=(mm if dc == DC - 1 else None))
                    if hc == HC - 1:
                        mm_after_u[i] = mm.n
                for hc in (hc0 - 3, hc0 - 2):
                    if hc >= 0:
                        emit_conv_tail(i, hc)
                p.wait("vector", mm, mm.n)
                for hc in (hc0, hc0 + 1):
                    bank = (ps_a, ps_b, ps_s[0], ps_s[1])[hc % 4]
                    p.op("vector", lambda v, hc=hc, bank=bank, off=off, W=W,
                         fb_s=fb_s: v.tensor_tensor(
                        out=ubuf[:, hc * T1W2 + off: hc * T1W2 + off + W],
                        in0=bank[:, 0:W],
                        in1=f1bc[:, fb_s + off:fb_s + off + W], op=ALU.mult),
                        inc=vv)
                    ev_u[(i, hc)] = vv.n
                    if off == 1:
                        p.op("vector", lambda v, hc=hc: v.memset(
                            ubuf[:, hc * T1W2:hc * T1W2 + 1], 0.0))
                    if off + W < T1W2:
                        p.op("vector", lambda v, hc=hc: v.memset(
                            ubuf[:, hc * T1W2 + off + W:hc * T1W2 + T1W2],
                            0.0))
                for hc in (hc0, hc0 + 1):
                    p.wait("scalar", vv, ev_u[(i, hc)])
                    p.op("scalar", lambda sc, hc=hc: sc.activation(
                        out=vt1[:, (hc % 4) * T1W2:(hc % 4) * T1W2 + T1],
                        in_=ubuf[:, hc * T1W2 + 1:hc * T1W2 + 1 + T1],
                        func=AF.Identity,
                        scale=cwcol[:, 3 * hc + 1:3 * hc + 2],
                        bias=cbcol[:, hc:hc + 1]), inc=ssm)
                    ss_vt1[(i, hc)] = ssm.n
            for tl in range(HC - 3, HC):
                emit_conv_tail(i, tl)
            if i >= 1:
                emit_store(i - 1)
            if i + 2 < NT1:
                emit_loads(i + 2)
            p.wait("vector", ssm, ss_silu[(i, HC - 1)])
            p.op("vector", lambda v: v.reduce_max(
                out=maxM[:, 0:T1],
                in_=_ap(ubuf, 1, [[HC * T1W2, 128], [1, T1], [T1W2, HC]]),
                axis=AX.X, apply_absolute_value=True), inc=vv)
            ev_mabs = vv.n
            p.wait("tensor", vv, ev_mabs)
            for j in range(NSB1):
                p.op("tensor", lambda t, j=j: t.matmul(
                    ps_c[:, j * 128:(j + 1) * 128],
                    maxM[:, j * 128:(j + 1) * 128], identf[:],
                    start=True, stop=True), inc=(mm if j == NSB1 - 1 else None))
            p.wait("vector", mm, mm.n)
            for j in range(NSB1):
                p.op("vector", lambda v, j=j: v.reduce_max(
                    out=cols[:, 0:1], in_=ps_c[:, j * 128:(j + 1) * 128],
                    axis=AX.X))
                p.op("vector", lambda v, j=j: v.tensor_scalar(
                    out=cols[:, 1:2], in0=cols[:, 0:1], scalar1=1e-5,
                    scalar2=None, op0=ALU.max))
                p.op("vector", lambda v, j=j: v.reciprocal(
                    out=cols[:, 2:3], in_=cols[:, 1:2]))
                p.op("vector", lambda v, j=j: v.tensor_scalar(
                    out=cols[:, 3 + j:4 + j], in0=cols[:, 2:3], scalar1=127.0,
                    scalar2=None, op0=ALU.mult))
                p.op("vector", lambda v, j=j: v.reciprocal(
                    out=cols[:, 8:9], in_=cols[:, 3 + j:4 + j]))
                p.op("vector", lambda v, i=i, j=j: v.tensor_scalar(
                    out=f2cols[:, i * NSB1 + j:i * NSB1 + j + 1],
                    in0=cols[:, 8:9], scalar1=wqcols[:, 3:4], scalar2=None,
                    op0=ALU.mult), inc=vv)
            ev_scol = vv.n
            p.wait("tensor", vv, ev_scol)
            for j in range(NSB1):
                p.op("tensor", lambda t, j=j: t.matmul(
                    ps_d[0:1, j * 128:(j + 1) * 128],
                    cols[:, 3 + j:4 + j], identf[:], start=True, stop=True),
                    inc=(mm if j == NSB1 - 1 else None))
            p.wait("vector", mm, mm.n)
            p.op("vector", lambda v: v.tensor_copy(
                out=rowbuf[0:1, 0:T1], in_=ps_d[0:1, 0:T1]), inc=vv)
            ev_row = vv.n
            p.wait("tensor", vv, ev_row)
            p.op("tensor", lambda t: t.matmul(
                ps_d[:, 0:T1], esel[:], rowbuf[:, 0:T1], start=True, stop=True),
                inc=mm)
            p.wait("vector", mm, mm.n)
            p.op("vector", lambda v: v.tensor_copy(
                out=s2bc8[:, 0:T1], in_=ps_d[:, 0:T1]), inc=vv)
            w = T1
            while w < NQG * T1:
                cw2 = min(w, NQG * T1 - w)
                p.op("vector", lambda v, w=w, cw2=cw2: v.tensor_copy(
                    out=s2bc8[:, w:w + cw2], in_=s2bc8[:, 0:cw2]))
                w += cw2
            p.op("vector", lambda v: v.engine_nop(), inc=vv)
        for g0 in range(0, HC, NQG):
            emit_quant_A(NT1 - 1, g0)
            emit_quant_B(NT1 - 1, g0)
        emit_store(NT1 - 1)
        p.run()

    quantize_weights(w2T, HC, D, 1)

    NSB1F = T1 // 128
    NHQS = 3
    with nc.sbuf_tensor("hq2", [128, NHQS * HC * T1], BF16) as hq2, \
         nc.sbuf_tensor("xres2", [128, 2 * D], F32) as xres2, \
         nc.sbuf_tensor("htile2", [128, D], F32) as htile2, \
         nc.sbuf_tensor("atile2", [128, D], F32) as atile2, \
         nc.sbuf_tensor("ln1t2", [128, D], F32) as ln1t2, \
         nc.sbuf_tensor("sqsc2", [128, D], F32) as sqsc2, \
         nc.sbuf_tensor("x1t2", [128, 2 * D], F32) as x1t2, \
         nc.sbuf_tensor("iqt2", [128, D], I32) as iqt2, \
         nc.sbuf_tensor("iqt2b", [128, D], I32) as iqt2b, \
         nc.sbuf_tensor("xqt2", [128, 2 * D], BF16) as xqt2:
        eb2 = (xres2, htile2, atile2, ln1t2, sqsc2, x1t2, iqt2, iqt2b, xqt2)
        p = PB(nc, "F2")
        p.wait("sync", dS, dS.n)
        p.wait("sync", dSb, dSb.n)
        p.wait("sync", dO, dO.n)
        p.wait("sync", dOb, dOb.n)
        p.wait("sync", dX, dX.n)
        p.wait("sync", dXb, dXb.n)
        hq_ld = {}
        mm_after_f2 = {}
        st_cnt2 = {}
        dO_bases2 = {0: dO.n, 1: dOb.n}
        ev_h1_list = []

        def load_hq(i):
            if i >= NHQS:
                p.wait("sync", mm, mm_after_f2[i - NHQS])
            sem = dS if i % 2 == 0 else dSb
            p.wait("sync", sem, sem.n)
            p.op("sync", lambda s, i=i: s.dma_start(
                out=_ap(hq2, (i % NHQS) * HC * T1,
                        [[NHQS * HC * T1, 128], [T1, HC], [1, T1]]),
                in_=_ap(hqs.tensor, i * HC * 128 * T1,
                        [[T1, 128], [128 * T1, HC], [1, T1]])), inc=sem, k=16)
            hq_ld[i] = (sem, sem.n)

        for i in range(min(NHQS, NT1)):
            load_hq(i)
        for i in range(NT1):
            slot = (i % NHQS) * HC * T1
            for sq in range(NSB1F):
                st = i * NSB1F + sq
                row0 = i * T1 + sq * 128
                ldsem = dX if st % 2 == 0 else dXb
                if st >= 2:
                    p.wait("sync", vv, ev_h1_list[st - 2])
                p.op("sync", lambda s, row0=row0, st=st: s.dma_start(
                    out=xres2[:, (st % 2) * D:(st % 2 + 1) * D],
                    in_=x1s[row0:row0 + 128, :]), inc=ldsem, k=16)
                if sq == 0:
                    sem, cnt = hq_ld[i]
                    p.wait("tensor", sem, cnt)
                if st >= 2:
                    p.wait("tensor", vv, ev_h1_list[st - 2])
                banks = (ps_c, ps_d) if st % 2 == 0 else (ps_s[0], ps_s[1])
                for hc in range(HC):
                    last = (hc == HC - 1)
                    p.op("tensor", lambda t, slot=slot, sq=sq, hc=hc,
                         banks=banks: t.matmul(
                        banks[0][:, 0:DH],
                        hq2[:, slot + hc * T1 + sq * 128:
                            slot + hc * T1 + sq * 128 + 128],
                        tw[:, hc * D + 0:hc * D + DH],
                        start=(hc == 0), stop=(hc == HC - 1)),
                        inc=(mm if (last and NDH == 1) else None))
                    if NDH > 1:
                        p.op("tensor", lambda t, slot=slot, sq=sq, hc=hc,
                             banks=banks: t.matmul(
                            banks[1][:, 0:DH],
                            hq2[:, slot + hc * T1 + sq * 128:
                                slot + hc * T1 + sq * 128 + 128],
                            tw[:, hc * D + DH:hc * D + 2 * DH],
                            start=(hc == 0), stop=(hc == HC - 1)),
                            inc=(mm if last else None))
                par = st % 2
                sto = dO if par == 0 else dOb
                dep = st_cnt2.get(st - 2, dO_bases2[par])
                ev = emit_epilogue(p, row0, banks[0], banks[1], mm.n, eb2,
                                   "ffn", par, dep_store=dep, sto=sto,
                                   ldsem=ldsem)
                ev_h1_list.append(ev)
                st_cnt2[st] = sto.n
            mm_after_f2[i] = mm.n
            if i + NHQS < NT1:
                load_hq(i + NHQS)
        p.run()

    p = PB(nc, "END")
    p.wait("sync", dS, dS.n)
    p.wait("sync", dSb, dSb.n)
    p.wait("sync", dX, dX.n)
    p.wait("sync", dXb, dXb.n)
    p.wait("scalar", dO, dO.n)
    p.wait("scalar", dOb, dOb.n)
    p.wait("gpsimd", dG, dG.n)
    p.run()

    stack.close()
    return nc


def host_inputs(cfg, core, x, w_qkv, w_fc, b_fc, gamma, beta, w1, conv_w,
                conv_b, w2):
    S, D, NSEQ = cfg.S, cfg.D, cfg.NSEQ
    b0 = core * NSEQ
    xs = x[b0:b0 + NSEQ]
    return {
        "x_tm": np.ascontiguousarray(xs.reshape(NSEQ * S, D)),
        "x_fm": np.ascontiguousarray(xs.transpose(0, 2, 1)),
        "wqT": np.ascontiguousarray(w_qkv[0:D].T),
        "wkT": np.ascontiguousarray(w_qkv[D:2 * D].T),
        "wvT": np.ascontiguousarray(w_qkv[2 * D:3 * D].T),
        "wfcT": np.ascontiguousarray(w_fc.T),
        "bfc": np.ascontiguousarray(b_fc.reshape(1, D)),
        "gamma": np.ascontiguousarray(gamma.reshape(1, D)),
        "beta": np.ascontiguousarray(beta.reshape(1, D)),
        "w1T": np.ascontiguousarray(w1.T),
        "w2T": np.ascontiguousarray(w2.T),
        "cw": np.ascontiguousarray(conv_w.reshape(cfg.H, 3)),
        "cb": np.ascontiguousarray(conv_b.reshape(cfg.H, 1)),
    }


def ref_forward(cfg, x, w_qkv, w_fc, b_fc, gamma, beta, w1, conv_w, conv_b,
                w2, dtype=np.float64):
    S, D, H = cfg.S, cfg.D, cfg.H
    f = lambda a: np.asarray(a, dtype=dtype)
    x, w_qkv, w_fc, b_fc = f(x), f(w_qkv), f(w_fc), f(b_fc)
    gamma, beta, w1, conv_w, conv_b, w2 = (f(gamma), f(beta), f(w1),
                                           f(conv_w), f(conv_b), f(w2))

    def ln(h):
        mu = h.mean(-1, keepdims=True)
        va = h.var(-1, keepdims=True)
        return (h - mu) / np.sqrt(va + cfg.LN_EPS)

    qkv = np.einsum("bsd,ed->bse", x, w_qkv)
    q, k, v = np.split(qkv, 3, axis=-1)
    s = np.einsum("bqd,bkd->bqk", q, k) * cfg.QSCALE
    s = s - s.max(-1, keepdims=True)
    a = np.exp(s)
    a /= a.sum(-1, keepdims=True)
    y = a.sum(-1, keepdims=True) * v.sum(1, keepdims=True)
    y = np.einsum("bqd,ed->bqe", y, w_fc) + b_fc
    h = x + y
    attn = ln(h) * gamma + beta
    x1 = ln(x + attn)

    def act_quant(t):
        sc = 127.0 / np.clip(np.abs(t).max(-1, keepdims=True), 1e-5, None)
        q = np.clip(np.round(t * sc), -128, 127)
        return q / sc

    def weight_quant(w):
        sc = 1.0 / np.clip(np.abs(w).mean(), 1e-5, None)
        return np.clip(np.round(w * sc), -1, 1) / sc

    hq = np.einsum("bsd,hd->bsh", act_quant(x1), weight_quant(w1))
    cwk = conv_w.reshape(H, 3)
    hp = np.pad(hq, ((0, 0), (1, 1), (0, 0)))
    conv = (hp[:, :-2, :] * cwk[:, 0] + hp[:, 1:-1, :] * cwk[:, 1] +
            hp[:, 2:, :] * cwk[:, 2]) + conv_b.reshape(H)
    sl = conv / (1.0 + np.exp(-conv))
    out = np.einsum("bsh,eh->bse", act_quant(sl), weight_quant(w2))
    x2 = ln(x1 + out)
    return x2


from concourse.bass_utils import run_bass_kernel_spmd

_N_CORES = 8


def kernel(L0_attn_x, L0_attn_w_qkv, L0_attn_w_fc, L0_attn_b_fc, L0_attn_gamma,
           L0_attn_beta, L0_ffn_w1, L0_ffn_conv_w, L0_ffn_conv_b, L0_ffn_w2):
    cfg = Cfg(S=2048, D=1024, H=4096, NSEQ=2, T1=256, T2=512)
    x = np.asarray(L0_attn_x)
    nc = build_program(cfg)
    in_maps = [
        host_inputs(cfg, c, x, np.asarray(L0_attn_w_qkv),
                    np.asarray(L0_attn_w_fc), np.asarray(L0_attn_b_fc),
                    np.asarray(L0_attn_gamma), np.asarray(L0_attn_beta),
                    np.asarray(L0_ffn_w1), np.asarray(L0_ffn_conv_w),
                    np.asarray(L0_ffn_conv_b), np.asarray(L0_ffn_w2))
        for c in range(_N_CORES)
    ]
    res = run_bass_kernel_spmd(nc, in_maps, core_ids=list(range(_N_CORES)))
    parts = [np.asarray(r["y"]).reshape(cfg.NSEQ, cfg.S, cfg.D)
             for r in res.results]
    return np.ascontiguousarray(np.concatenate(parts, axis=0), dtype=np.float32)
